# version 07
# baseline (speedup 1.0000x reference)
import numpy as np
from contextlib import ExitStack

import concourse.bass as bass
import concourse.mybir as mybir
from concourse.bass_utils import run_bass_kernel_spmd

F32 = mybir.dt.float32
BF16 = mybir.dt.bfloat16
F8 = mybir.dt.float8e4
AF = mybir.ActivationFunctionType
ALU = mybir.AluOpType

NCORES = 8
B, T, D, HID, HEADS, HD = 4, 2048, 2048, 8192, 16, 128
TOK = 1024
NTT = TOK // 128
FC = D // 128
HC = HID // 128
L = 8
NCH = TOK // L
NB = 16
L2 = 8
EPS = 1e-5

ENGINES = ("sync", "gpsimd", "vector", "scalar", "tensor")
COMPUTE_SEM_ENGINE = {"sT": "tensor", "sV": "vector", "sS": "scalar",
                      "sG": "gpsimd"}


class Sem:
    def __init__(self, ctx, nc, name):
        self.h = ctx.enter_context(nc.semaphore(name))
        self.n = 0
        self.name = name

    def bump(self, k=1):
        self.n += k
        return self.n


class Builder:

    def __init__(self):
        self.q = {e: [] for e in ENGINES}
        self.last_wait = {e: {} for e in ENGINES}
        self.res = {}

    def _deps_for(self, reads, writes):
        deps = {}
        for r in reads:
            e = self.res.get(r)
            if e:
                for s, v in e["w"].items():
                    if deps.get(s, -1) < v:
                        deps[s] = v
        for r in writes:
            e = self.res.get(r)
            if e:
                for part in ("w", "r"):
                    for s, v in e[part].items():
                        if deps.get(s, -1) < v:
                            deps[s] = v
        return deps

    def _record(self, reads, writes, sem, val):
        for r in reads:
            e = self.res.setdefault(r, {"w": {}, "r": {}})
            if e["r"].get(sem, -1) < val:
                e["r"][sem] = val
        for r in writes:
            e = self.res.setdefault(r, {"w": {}, "r": {}})
            e["w"] = {sem: val}
            e["r"] = {}

    def emit(self, eng, make, sem, k=1, reads=(), writes=(), extra_deps=(),
             quiet=False):
        deps = self._deps_for(reads, writes)
        for s, v in extra_deps:
            if deps.get(s, -1) < v:
                deps[s] = v
        eff = []
        lw = self.last_wait[eng]
        for s, v in deps.items():
            if lw.get(id(s), -1) < v:
                eff.append((s.h, v))
                lw[id(s)] = v
        if quiet:
            def fnq(e, _make=make, _eff=tuple(eff)):
                for h, v in _eff:
                    e.wait_ge(h, v)
                _make(e)
            self.q[eng].append(fnq)
            return None
        val = sem.bump(k)

        def fn(e, _make=make, _eff=tuple(eff), _s=sem.h, _k=k):
            for h, v in _eff:
                e.wait_ge(h, v)
            inst = _make(e)
            inst.then_inc(_s, _k)
        self.q[eng].append(fn)
        self._record(reads, writes, sem, val)
        return val

    def dma(self, eng, sem, out_ap, in_ap, reads=(), writes=(), extra_deps=()):
        return self.emit(eng, lambda e, _o=out_ap, _i=in_ap: e.dma_start(
            out=_o, in_=_i), sem, k=16, reads=reads, writes=writes,
            extra_deps=extra_deps)

    def barrier_wait(self, eng, extra_deps):
        lw = self.last_wait[eng]
        eff = []
        for s, v in extra_deps:
            if lw.get(id(s), -1) < v:
                eff.append((s.h, v))
                lw[id(s)] = v
        if eff:
            def fn(e, _eff=tuple(eff)):
                for h, v in _eff:
                    e.wait_ge(h, v)
            self.q[eng].append(fn)


def build(debug_tap=None, upto="all"):
    nc = bass.Bass("TRN2", target_bir_lowering=False)

    x_in = nc.dram_tensor("x", [TOK, D], F32, kind="ExternalInput").ap()
    in_w = nc.dram_tensor("in_w", [D, D], F32, kind="ExternalInput").ap()
    out_w = nc.dram_tensor("out_w", [D, D], F32, kind="ExternalInput").ap()
    w1 = nc.dram_tensor("w1", [D, HID], F32, kind="ExternalInput").ap()
    w2 = nc.dram_tensor("w2", [HID, D], F32, kind="ExternalInput").ap()
    wg = nc.dram_tensor("wg", [2, D, HID], F32, kind="ExternalInput").ap()
    wlin = nc.dram_tensor("wlin", [HID, D], F32, kind="ExternalInput").ap()
    A_in = nc.dram_tensor("A", [HEADS, HD, HD], F32, kind="ExternalInput").ap()
    B_in = nc.dram_tensor("Bm", [HEADS, HD, HD], F32, kind="ExternalInput").ap()
    C_in = nc.dram_tensor("Cm", [HEADS, HD, HD], F32, kind="ExternalInput").ap()
    D_in = nc.dram_tensor("Dm", [HEADS, HD, HD], F32, kind="ExternalInput").ap()
    ln1w = nc.dram_tensor("ln1w", [128, FC], F32, kind="ExternalInput").ap()
    ln1b = nc.dram_tensor("ln1b", [128, FC], F32, kind="ExternalInput").ap()
    ln2w = nc.dram_tensor("ln2w", [128, FC], F32, kind="ExternalInput").ap()
    ln2b = nc.dram_tensor("ln2b", [128, FC], F32, kind="ExternalInput").ap()
    inb = nc.dram_tensor("inb", [128, FC], F32, kind="ExternalInput").ap()
    outb = nc.dram_tensor("outb", [128, FC], F32, kind="ExternalInput").ap()
    b1v = nc.dram_tensor("b1v", [128, HC], F32, kind="ExternalInput").ap()
    b2v = nc.dram_tensor("b2v", [128, FC], F32, kind="ExternalInput").ap()
    mask_in = nc.dram_tensor("mask", [128, 8], F32, kind="ExternalInput").ap()

    out_t = nc.dram_tensor("out", [TOK, D], F32, kind="ExternalOutput").ap()

    spill_x = nc.dram_tensor("spill_x", [TOK, D], F32).ap()
    spill_xp = nc.dram_tensor("spill_xp", [TOK, D], F32).ap()
    spill_xa2 = nc.dram_tensor("spill_xa2", [TOK, D], F32).ap()
    spill_mh = nc.dram_tensor("spill_mh", [2, 128, FC * TOK], BF16).ap()
    mst_d = nc.dram_tensor("mst", [HEADS, 2, 128, L * 128], BF16).ap()
    gst_d = nc.dram_tensor("gst", [HEADS, 2, 128, L * 128], BF16).ap()
    wst_d = nc.dram_tensor("wst", [HEADS, 2, 128, L * 128], BF16).ap()
    r8_d = nc.dram_tensor("r8", [HEADS, 2, 128, 7 * 128], BF16).ap()
    dbg_d = nc.dram_tensor("dbg", [128, 4096], F32).ap()
    cc_in_d = nc.dram_tensor("cc_in", [128, HEADS], F32)
    cc_out_d = nc.dram_tensor("cc_out", [NCORES, 128, HEADS], F32)


    with ExitStack() as ctx:
        def sbt(name, shape, dt):
            return ctx.enter_context(nc.sbuf_tensor(name, shape, dt))

        def psm(name, shape, dt):
            return ctx.enter_context(nc.psum_tensor(name, shape, dt))

        R1 = sbt("R1", [128, 16384], F32)
        R2 = sbt("R2", [128, 16384], F32)
        WARE = sbt("WARE", [128, 8192], F32)
        HQ = sbt("HQ", [128, 16384], F8)
        NORM = sbt("NORM", [128, 2048], F32)
        JUNK = sbt("JUNK", [128, 2048], F8)
        STAT = sbt("STAT", [128, 24], F32)
        IDN = sbt("IDN", [128, 128], F32)
        IDNB = sbt("IDNB", [128, 128], BF16)
        CONST = sbt("CONST", [128, 8 * FC + HC + 8], F32)
        GF = sbt("GF", [128, 1536], BF16)
        STAGE = sbt("STAGE", [128, 1536], F32)
        SMALL = sbt("SMALL", [128, 64], F32)
        SSMS = sbt("SSMS", [128, 960], F32)
        SSP = sbt("SSP", [128, 544], BF16)
        XT2 = sbt("XT2", [128, 1024], F32)

        PS_A = psm("PS_A", [128, 1024], F32)
        PS_B = psm("PS_B", [128, 512], F32)
        PS_C = psm("PS_C", [128, 512], F32)
        PS_D = psm("PS_D", [128, 512], F32)
        PS_E = psm("PS_E", [128, 512], F32)
        PS_F = psm("PS_F", [128, 512], F32)
        PS_G = psm("PS_G", [128, 512], BF16)

        dS = Sem(ctx, nc, "dS")
        dG = Sem(ctx, nc, "dG")
        dX = Sem(ctx, nc, "dX")
        dC = Sem(ctx, nc, "dC")
        dW0 = Sem(ctx, nc, "dW0")
        dW1 = Sem(ctx, nc, "dW1")
        dV0 = Sem(ctx, nc, "dV0")
        dV1 = Sem(ctx, nc, "dV1")
        dM0 = Sem(ctx, nc, "dM0")
        dM1 = Sem(ctx, nc, "dM1")
        dPL = Sem(ctx, nc, "dPL")
        dPS0 = Sem(ctx, nc, "dPS0")
        dPS1 = Sem(ctx, nc, "dPS1")
        dSpX = Sem(ctx, nc, "dSpX")
        dSpP = Sem(ctx, nc, "dSpP")
        dSpA = Sem(ctx, nc, "dSpA")
        dSpM0 = Sem(ctx, nc, "dSpM0")
        dSpM1 = Sem(ctx, nc, "dSpM1")
        dSpM2 = Sem(ctx, nc, "dSpM2")
        dSpM3 = Sem(ctx, nc, "dSpM3")
        dR0 = Sem(ctx, nc, "dR0")
        dR1 = Sem(ctx, nc, "dR1")
        dCc = Sem(ctx, nc, "dCc")
        sCC = Sem(ctx, nc, "sCC")
        sT = Sem(ctx, nc, "sT")
        sV = Sem(ctx, nc, "sV")
        sS = Sem(ctx, nc, "sS")
        sG = Sem(ctx, nc, "sG")
        bld = Builder()

        c_ln1w = CONST[:, 0 * FC:1 * FC]
        c_ln1b = CONST[:, 1 * FC:2 * FC]
        c_ln2w = CONST[:, 2 * FC:3 * FC]
        c_ln2b = CONST[:, 3 * FC:4 * FC]
        c_inb = CONST[:, 4 * FC:5 * FC]
        c_outb = CONST[:, 5 * FC:6 * FC]
        c_b2 = CONST[:, 6 * FC:7 * FC]
        c_b1 = CONST[:, 7 * FC:7 * FC + HC]
        c_mask = CONST[:, 7 * FC + HC:7 * FC + HC + 8]

        bld.dma("sync", dX, R1[:].rearrange("p (tt d) -> p tt d", d=D),
                x_in.rearrange("(tt p) d -> p tt d", p=128), writes=["R1"])
        for dst, srcv in ((c_ln1w, ln1w), (c_ln1b, ln1b), (c_ln2w, ln2w),
                          (c_ln2b, ln2b), (c_inb, inb), (c_outb, outb),
                          (c_b2, b2v), (c_b1, b1v), (c_mask, mask_in)):
            bld.dma("sync", dC, dst, srcv[:, :], writes=["CONST"])
        bld.emit("gpsimd", lambda g: g.iota(
            STAGE[:, 0:128], [[1, 128]], channel_multiplier=0,
            allow_small_or_imprecise_dtypes=True), sG, writes=["STAGE"])
        bld.emit("gpsimd", lambda g: g.iota(
            STAGE[:, 128:256], [[0, 128]], channel_multiplier=1,
            allow_small_or_imprecise_dtypes=True), sG, writes=["STAGE"])
        bld.emit("vector", lambda v: v.tensor_tensor(
            IDN[:], STAGE[:, 0:128], STAGE[:, 128:256], ALU.is_equal), sV,
            reads=["STAGE"], writes=["IDN"])
        bld.emit("vector", lambda v: v.tensor_copy(IDNB[:], IDN[:]), sV,
                 reads=["IDN"], writes=["IDNB"])

        SC16 = 2.0 ** -16
        def ln_normalize(X_ap, xres, out_ap=None, out_res="NORM", par=0):
            if out_ap is None:
                out_ap = NORM[:, :]
            st = STAT[:, par * 12:par * 12 + 12]
            stres = f"STAT{par}"
            jres = f"JUNK{par}" 
            bld.emit("vector", lambda v: v.tensor_reduce(
                st[:, 0:1], X_ap, mybir.AxisListType.X, ALU.add), sV,
                reads=[xres], writes=[stres])
            bld.emit("vector", lambda v: v.tensor_scalar(
                st[:, 2:3], st[:, 0:1], 1.0 / D, 0.0, ALU.mult, ALU.add), sV,
                reads=[stres], writes=[stres])
            bld.emit("vector", lambda v: v.tensor_scalar(
                st[:, 3:4], st[:, 2:3], -SC16, 0.0, ALU.mult, ALU.add), sV,
                reads=[stres], writes=[stres])
            bld.emit("scalar", lambda s: s.activation(
                JUNK[:, :], X_ap, AF.Square, bias=st[:, 3:4], scale=SC16,
                accum_out=st[:, 1:2]), sS,
                reads=[xres, stres], writes=[stres, "JUNK"])
            bld.emit("vector", lambda v: v.tensor_scalar(
                st[:, 4:5], st[:, 1:2], 1.0 / D, EPS * SC16 * SC16,
                ALU.mult, ALU.add), sV,
                reads=[stres], writes=[stres])
            bld.emit("scalar", lambda s: s.activation(
                st[:, 6:7], st[:, 4:5], AF.Sqrt), sS,
                reads=[stres], writes=[stres])
            bld.emit("vector", lambda v: v.reciprocal(
                st[:, 5:6], st[:, 6:7]), sV, reads=[stres], writes=[stres])
            bld.emit("vector", lambda v: v.tensor_scalar(
                st[:, 7:8], st[:, 5:6], SC16, 0.0, ALU.mult, ALU.add), sV,
                reads=[stres], writes=[stres])
            bld.emit("vector", lambda v: v.tensor_tensor(
                st[:, 8:9], st[:, 2:3], st[:, 7:8], ALU.mult), sV,
                reads=[stres], writes=[stres])
            bld.emit("vector", lambda v: v.tensor_scalar(
                st[:, 9:10], st[:, 8:9], -1.0, 0.0, ALU.mult, ALU.add), sV,
                reads=[stres], writes=[stres])
            bld.emit("scalar", lambda s: s.activation(
                out_ap, X_ap, AF.Identity, bias=st[:, 9:10],
                scale=st[:, 7:8]), sS, reads=[xres, stres], writes=[out_res])

        def ffn_sublayer(xbuf, xres, pbuf, prodres):
            x3 = xbuf[:].rearrange("p (tt d) -> p tt d", d=D)
            prod = pbuf[:].bitcast(F8)
            w8 = WARE[:].bitcast(F8)
            for tt in range(NTT):
                ln_normalize(x3[:, tt, :], xres, par=tt % 2)
                for c4 in range(4):
                    for q in range(4):
                        c = c4 * 4 + q
                        bld.emit("tensor", lambda t, _c=c, _q=q: t.transpose(
                            PS_F[:, _q * 128:(_q + 1) * 128],
                            NORM[:, _c * 128:(_c + 1) * 128], IDN[:]), sT,
                            reads=["NORM", "IDN"], writes=["PS_F"],
                            quiet=(q != 3))
                    for q in range(4):
                        c = c4 * 4 + q
                        col = c * TOK + tt * 128
                        bld.emit("vector", lambda v, _col=col, _q=q: v.tensor_copy(
                            HQ[:, _col:_col + 128],
                            PS_F[:, _q * 128:(_q + 1) * 128]), sV,
                            reads=["PS_F"], writes=["HQ"])

            slot_cols = FC * 512
            src_g = wg.rearrange("g (kc p) m -> g p kc m", p=128)

            def load_gate_slab(mg, slot):
                for g in range(2):
                    bld.dma("gpsimd", (dW0, dW1)[slot],
                            w8[:, (2 * slot + g) * slot_cols:
                               (2 * slot + g + 1) * slot_cols]
                            .rearrange("p (kc m) -> p kc m", m=512),
                            src_g[g, :, :, mg * 512:(mg + 1) * 512],
                            writes=[f"WS{slot}"])

            load_gate_slab(0, 0)
            for mg in range(HID // 512):
                slot = mg % 2
                if mg + 1 < HID // 512:
                    load_gate_slab(mg + 1, (mg + 1) % 2)
                for mc4 in range(4):
                    hc_idx = mg * 4 + mc4
                    for g in range(2):
                        for kc in range(FC):
                            lhs = w8[:, (2 * slot + g) * slot_cols + kc * 512
                                     + mc4 * 128:(2 * slot + g) * slot_cols
                                     + kc * 512 + mc4 * 128 + 128]
                            for nt in range(2):
                                psg, gres = (PS_A[:, 0:512], "PS_A0") if nt == 0 else (PS_B[:, :], "PS_B")
                                psf, fres = (PS_A[:, 512:1024], "PS_A1") if nt == 0 else (PS_C[:, :], "PS_C")
                                ps, qres = (psg, gres) if g == 0 else (psf, fres)
                                rhs = HQ[:, kc * TOK + nt * 512:
                                         kc * TOK + nt * 512 + 512]
                                bld.emit("tensor", lambda t, _l=lhs, _r=rhs,
                                         _p=ps, _k=kc: t.matmul(
                                             _p, _l, _r, start=(_k == 0),
                                             stop=(_k == FC - 1)), sT,
                                         reads=[f"WS{slot}", "HQ"],
                                         writes=[qres] if kc == 0 else [qres, qres + "acc"],
                                         quiet=(kc != FC - 1))
                    for nt in range(2):
                        psg, gres = (PS_A[:, 0:512], "PS_A0") if nt == 0 else (PS_B[:, :], "PS_B")
                        psf, fres = (PS_A[:, 512:1024], "PS_A1") if nt == 0 else (PS_C[:, :], "PS_C")
                        bld.emit("scalar", lambda s, _p=psg: s.activation(
                            GF[:, 0:512], _p, AF.Gelu_apprx_tanh), sS,
                            reads=[gres, gres + "acc"], writes=["GF0"])
                        bld.emit("vector", lambda v, _p=psf: v.tensor_copy(
                            GF[:, 512:1024], _p), sV,
                            reads=[fres, fres + "acc"], writes=["GF1"])
                        bld.emit("vector", lambda v: v.tensor_tensor(
                            GF[:, 1024:1536], GF[:, 0:512], GF[:, 512:1024],
                            ALU.mult), sV, reads=["GF0", "GF1"], writes=["GF2"])
                        col = hc_idx * TOK + nt * 512
                        bld.emit("vector", lambda v, _col=col: v.tensor_copy(
                            prod[:, _col:_col + 512], GF[:, 1024:1536]), sV,
                            reads=["GF2"], writes=[prodres])

            lin_cols = HC * 256
            src_l = wlin.rearrange("(kc p) m -> p kc m", p=128)

            def load_lin_slab(mg2, slot):
                bld.dma("gpsimd", (dW0, dW1)[slot],
                        w8[:, slot * lin_cols:(slot + 1) * lin_cols]
                        .rearrange("p (kc m) -> p kc m", m=256),
                        src_l[:, :, mg2 * 256:(mg2 + 1) * 256],
                        writes=[f"WS{slot}"])

            load_lin_slab(0, 0)
            for mg2 in range(D // 256):
                slot = mg2 % 2
                if mg2 + 1 < D // 256:
                    load_lin_slab(mg2 + 1, (mg2 + 1) % 2)
                for mc2 in range(2):
                    fc_idx = mg2 * 2 + mc2
                    for kc in range(HC):
                        lhs = w8[:, slot * lin_cols + kc * 256 + mc2 * 128:
                                 slot * lin_cols + kc * 256 + mc2 * 128 + 128]
                        for nt in range(2):
                            ps, bres = ((PS_D, "PS_D"), (PS_E, "PS_E"))[nt]
                            rhs = prod[:, kc * TOK + nt * 512:
                                       kc * TOK + nt * 512 + 512]
                            bld.emit("tensor", lambda t, _l=lhs, _r=rhs,
                                     _p=ps, _k=kc: t.matmul(
                                         _p[:, :], _l, _r, start=(_k == 0),
                                         stop=(_k == HC - 1)), sT,
                                     reads=[f"WS{slot}", prodres],
                                     writes=[bres] if kc == 0 else [bres, bres + "acc"],
                                     quiet=(kc != HC - 1))
                    for nt in range(2):
                        ps, bres = ((PS_D, "PS_D"), (PS_E, "PS_E"))[nt]
                        bld.emit("vector", lambda v, _p=ps: v.tensor_copy(
                            GF[:, 0:512], _p[:, :]), sV,
                            reads=[bres, bres + "acc"], writes=["GF0"])
                        for q in range(4):
                            bld.emit("tensor", lambda t, _q=q: t.transpose(
                                PS_G[:, _q * 128:(_q + 1) * 128],
                                GF[:, _q * 128:(_q + 1) * 128], IDNB[:]), sT,
                                reads=["GF0", "IDNB"], writes=["PS_G"],
                                quiet=(q != 3))
                        for q in range(4):
                            bld.emit("scalar", lambda s, _q=q: s.activation(
                                STAGE[:, _q * 128:(_q + 1) * 128],
                                PS_G[:, _q * 128:(_q + 1) * 128],
                                AF.Identity, scale=0.5), sS,
                                reads=["PS_G"], writes=["STAGE"])
                        for q in range(4):
                            tt = nt * 4 + q
                            bld.emit("vector", lambda v, _tt=tt, _fc=fc_idx,
                                     _q=q: v.tensor_tensor(
                                         x3[:, _tt, _fc * 128:(_fc + 1) * 128],
                                         x3[:, _tt, _fc * 128:(_fc + 1) * 128],
                                         STAGE[:, _q * 128:(_q + 1) * 128],
                                         ALU.add), sV,
                                     reads=["STAGE", xres], writes=[xres])

        ffn_sublayer(R1, "R1", R2, "R2")

        R1b = R1[:].bitcast(BF16)
        R2b = R2[:].bitcast(BF16)
        HQ32 = HQ[:].bitcast(F32)
        HQb = HQ[:]
        HQbf = HQ[:].bitcast(BF16)
        WAREb = WARE[:].bitcast(BF16)
        JUNKb = JUNK[:].bitcast(BF16)

        def prologue():
            a_t = STAGE[:, 0:128]; b_t = STAGE[:, 128:256]
            c_t = STAGE[:, 256:384]; d_t = STAGE[:, 384:512]
            CTB = STAGE[:, 512:640]; N8 = STAGE[:, 768:896]
            RSTK = HQ32[:, 0:1024]; GSTK = HQ32[:, 1024:2048]
            SC8 = HQ32[:, 2048:4096]
            WSTKF = NORM[:, 0:1024]; MSTKF = NORM[:, 1024:2048]

            def chain_mm(lhs_ap, rhs_ap, out_sb_ap, pidx, extra=(),
                         group=None):
                ps, pres = ((PS_D, "PS_D"), (PS_E, "PS_E"))[pidx % 2]
                if group is None:
                    bld.emit("tensor", lambda t, _l=lhs_ap, _r=rhs_ap,
                             _p=ps: t.matmul(_p[:, 0:128], _l, _r,
                                             start=True, stop=True), sT,
                             reads=list(extra) + ["STAGE", "HQ", "NORM"],
                             writes=[pres])
                else:
                    for gi, (la, ra) in enumerate(group):
                        bld.emit("tensor", lambda t, _l=la, _r=ra, _p=ps,
                                 _s=(gi == 0), _e=(gi == len(group) - 1):
                                 t.matmul(_p[:, 0:128], _l, _r,
                                          start=_s, stop=_e), sT,
                                 reads=list(extra) + ["STAGE", "HQ", "NORM"],
                                 writes=[pres])
                bld.emit("scalar", lambda s, _o=out_sb_ap, _p=ps: s.copy(
                    _o, _p[:, 0:128]), sS, reads=[pres],
                    writes=[out_sb_ap_res])

            src_abcd = [A_in, B_in, C_in, D_in]
            for h in range(HEADS):
                for i4, srcm in enumerate(src_abcd):
                    bld.dma("sync", dPL, STAGE[:, i4 * 128:(i4 + 1) * 128],
                            srcm[h, :, :], writes=["STAGE"])
                pidx = 0
                global out_sb_ap_res
                out_sb_ap_res = "STAGE"
                chain_mm(c_t, IDN[:], CTB, pidx, extra=["IDN"]); pidx += 1
                out_sb_ap_res = "HQ"
                bld.emit("vector", lambda v: v.tensor_copy(
                    RSTK[:, 0:128], IDN[:]), sV, reads=["IDN"], writes=["HQ"])
                chain_mm(a_t, CTB, GSTK[:, 0:128], pidx); pidx += 1
                for m in range(1, 8):
                    chain_mm(a_t, RSTK[:, (m - 1) * 128:m * 128],
                             RSTK[:, m * 128:(m + 1) * 128], pidx); pidx += 1
                    chain_mm(a_t, GSTK[:, (m - 1) * 128:m * 128],
                             GSTK[:, m * 128:(m + 1) * 128], pidx); pidx += 1
                out_sb_ap_res = "NORM"
                chain_mm(None, None, MSTKF[:, 0:128], pidx,
                         group=[(b_t, CTB), (d_t, IDN[:])],
                         extra=["IDN"]); pidx += 1
                for j in range(8):
                    chain_mm(b_t, RSTK[:, (7 - j) * 128:(8 - j) * 128],
                             WSTKF[:, j * 128:(j + 1) * 128], pidx); pidx += 1
                    if j >= 1:
                        chain_mm(b_t, GSTK[:, (j - 1) * 128:j * 128],
                                 MSTKF[:, j * 128:(j + 1) * 128], pidx); pidx += 1
                def split_store(stk_ap, stk_res, dst, n):
                    bld.emit("vector", lambda v, _s=stk_ap, _n=n: v.tensor_copy(
                        GF[:, 0:_n], _s), sV, reads=[stk_res],
                        writes=["GF0"])
                    bld.dma("sync", dPS0, dst[0, :, :], GF[:, 0:n],
                            reads=["GF0"])
                    bld.emit("vector", lambda v, _s=stk_ap, _n=n: v.tensor_tensor(
                        JUNKb[:, 0:_n], _s, GF[:, 0:_n], ALU.subtract), sV,
                        reads=[stk_res, "GF0"], writes=["JUNK"])
                    bld.dma("sync", dPS1, dst[1, :, :], JUNKb[:, 0:n],
                            reads=["JUNK"])
                split_store(WSTKF[:, 0:1024], "NORM", wst_d[h], 1024)
                split_store(MSTKF[:, 0:1024], "NORM", mst_d[h], 1024)
                split_store(GSTK[:, 0:1024], "HQ", gst_d[h], 1024)
                out_sb_ap_res = "STAGE"
                chain_mm(RSTK[:, 7 * 128:8 * 128], a_t, N8, pidx); pidx += 1
                out_sb_ap_res = "NORM"
                chain_mm(a_t, RSTK[:, 7 * 128:8 * 128],
                         NORM[:, 0:128], pidx); pidx += 1
                for m2 in range(2, 8):
                    chain_mm(N8, NORM[:, (m2 - 2) * 128:(m2 - 1) * 128],
                             NORM[:, (m2 - 1) * 128:m2 * 128], pidx); pidx += 1
                out_sb_ap_res = "HQ"
                chain_mm(N8, NORM[:, 6 * 128:7 * 128],
                         SC8[:, h * 128:(h + 1) * 128], pidx); pidx += 1
                split_store(NORM[:, 0:896], "NORM", r8_d[h], 896)

        prologue()

        x3 = R1[:].rearrange("p (tt d) -> p tt d", d=D)
        xp3 = R2[:].rearrange("p (tt d) -> p tt d", d=D)
        for tt in range(NTT):
            ln_normalize(x3[:, tt, :], "R1", out_ap=xp3[:, tt, :],
                         out_res="R2", par=tt % 2)
        bld.dma("sync", dSpX, spill_x.rearrange("(tt p) d -> p tt d", p=128),
                x3[:, :, :], reads=["R1"])

        h1_hi = R1b[:, 0:16384]
        h1_lo = R1b[:, 16384:32768]
        for tt in range(NTT):
            ln_normalize(xp3[:, tt, :], "R2", par=tt % 2)
            for c4 in range(4):
                for q in range(4):
                    c = c4 * 4 + q
                    bld.emit("tensor", lambda t, _c=c, _q=q: t.transpose(
                        PS_F[:, _q * 128:(_q + 1) * 128],
                        NORM[:, _c * 128:(_c + 1) * 128], IDN[:]), sT,
                        reads=["NORM", "IDN"], writes=["PS_F"],
                        quiet=(q != 3))
                for q in range(4):
                    c = c4 * 4 + q
                    col = c * TOK + tt * 128
                    bld.emit("vector", lambda v, _c=c, _q=q: v.tensor_scalar(
                        STAGE[:, 1024 + _q * 128:1024 + (_q + 1) * 128],
                        PS_F[:, _q * 128:(_q + 1) * 128],
                        c_ln1w[:, _c:_c + 1], c_ln1b[:, _c:_c + 1],
                        ALU.mult, ALU.add), sV,
                        reads=["PS_F", "CONST"], writes=["STAGE"])
                for q in range(4):
                    c = c4 * 4 + q
                    col = c * TOK + tt * 128
                    bld.emit("scalar", lambda s, _col=col, _q=q: s.copy(
                        h1_hi[:, _col:_col + 128],
                        STAGE[:, 1024 + _q * 128:1024 + (_q + 1) * 128]), sS,
                        reads=["STAGE"], writes=["R1"])
                    bld.emit("vector", lambda v, _col=col, _q=q: v.tensor_tensor(
                        h1_lo[:, _col:_col + 128],
                        STAGE[:, 1024 + _q * 128:1024 + (_q + 1) * 128],
                        h1_hi[:, _col:_col + 128], ALU.subtract), sV,
                        reads=["STAGE", "R1"], writes=["R1"])
        bld.dma("sync", dSpP, spill_xp.rearrange("(tt p) d -> p tt d", p=128),
                xp3[:, :, :], reads=["R2"])

        def stream_w_pair(src3_ap, slot, kc_cnt=FC):
            n = kc_cnt * 128
            WSTG = WARE[:, slot * 4096:slot * 4096 + n]
            whi = WAREb[:, slot * 8192 + 4096:slot * 8192 + 4096 + n]
            wlo = WAREb[:, slot * 8192 + 6144:slot * 8192 + 6144 + n]
            bld.dma("sync", (dV0, dV1)[slot],
                    WSTG.rearrange("p (kc m) -> p kc m", m=128), src3_ap,
                    writes=[f"WS{slot}"])
            bld.emit("scalar", lambda s: s.copy(whi, WSTG), sS,
                     reads=[f"WS{slot}"], writes=[f"WS{slot}"])
            bld.emit("vector", lambda v: v.tensor_tensor(
                wlo, WSTG, whi, ALU.subtract), sV,
                reads=[f"WS{slot}"], writes=[f"WS{slot}"])
            return whi, wlo

        u_hi = R2b[:, 0:16384]
        u_lo = R2b[:, 16384:32768]
        src_iw = in_w.rearrange("(kc p) m -> p kc m", p=128)
        iw_pairs = {0: stream_w_pair(src_iw[:, :, 0:128], 0)}
        for h in range(HEADS):
            slot = h % 2
            if h + 1 < HEADS:
                iw_pairs[h + 1] = stream_w_pair(
                    src_iw[:, :, (h + 1) * 128:(h + 2) * 128], (h + 1) % 2)
            whi, wlo = iw_pairs.pop(h)
            mm_i = 0
            for pa, pb in ((whi, h1_hi), (whi, h1_lo), (wlo, h1_hi)):
                for kc in range(FC):
                    for nt in range(2):
                        ps, pres = ((PS_D, "PS_D"), (PS_E, "PS_E"))[nt]
                        bld.emit("tensor", lambda t, _pa=pa, _pb=pb, _kc=kc,
                                 _p=ps, _s=(mm_i <= 1), _e=(mm_i >= 94),
                                 _nt=nt: t.matmul(
                                     _p[:, :],
                                     _pa[:, _kc * 128:(_kc + 1) * 128],
                                     _pb[:, _kc * TOK + _nt * 512:
                                         _kc * TOK + _nt * 512 + 512],
                                     start=_s, stop=_e), sT,
                                 reads=[f"WS{slot}", "R1"],
                                 writes=[pres] if mm_i <= 1 else [pres, pres + "a"],
                                 quiet=(mm_i < 94))
                        mm_i += 1
            for nt in range(2):
                ps, pres = ((PS_D, "PS_D"), (PS_E, "PS_E"))[nt]
                bld.emit("vector", lambda v, _p=ps, _h=h: v.tensor_scalar(
                    STAGE[:, 0:512], _p[:, :], c_inb[:, _h:_h + 1], 0.0,
                    ALU.add, ALU.add), sV,
                    reads=[pres, pres + "a", "CONST"], writes=["STAGE"])
                colu = h * TOK + nt * 512
                bld.emit("scalar", lambda s, _c=colu: s.copy(
                    u_hi[:, _c:_c + 512], STAGE[:, 0:512]), sS,
                    reads=["STAGE"], writes=["R2"])
                bld.emit("vector", lambda v, _c=colu: v.tensor_tensor(
                    u_lo[:, _c:_c + 512], STAGE[:, 0:512],
                    u_hi[:, _c:_c + 512], ALU.subtract), sV,
                    reads=["STAGE", "R2"], writes=["R2"])

        bld.barrier_wait("sync", [(dPS0, dPS0.n), (dPS1, dPS1.n)])
        y32 = R1
        HH = NORM
        s32 = SSMS[:, 0:128]
        bstk = SSMS[:, 128:384]
        bstk2 = SSMS[:, 384:640]
        hin = SSMS[:, 640:656]
        hinraw = SSMS[:, 656:784]
        estk = SSMS[:, 784:800]
        h032 = SSMS[:, 800:928]
        s_hi = SSP[:, 0:128]; s_lo = SSP[:, 128:256]
        e_hi = SSP[:, 256:272]; e_lo = SSP[:, 272:288]
        h0_hi = SSP[:, 288:416]; h0_lo = SSP[:, 416:544]

        def mats_ap(h, which, hl):
            base = (h % 2) * 8192
            if which == "r8":
                off = 6144 + hl * 896
                return WAREb[:, base + off:base + off + 896]
            off = {"m": 0, "g": 2048, "w": 4096}[which] + hl * 1024
            return WAREb[:, base + off:base + off + 1024]

        def load_mats(h, full=True):
            slot = h % 2
            sem = (dM0, dM1)[slot]
            pairs = [("m", mst_d), ("g", gst_d), ("w", wst_d)] if full else [("g", gst_d)]
            for which, dsrc in pairs:
                base = (h % 2) * 8192 + {"m": 0, "g": 2048, "w": 4096}[which]
                bld.dma("sync", sem,
                        WAREb[:, base:base + 2048].rearrange(
                            "p (s c) -> p s c", c=1024),
                        dsrc[h].rearrange("s p c -> p s c"),
                        writes=[f"WS{slot}"])
            base = (h % 2) * 8192 + 6144
            bld.dma("sync", sem,
                    WAREb[:, base:base + 1792].rearrange(
                        "p (s c) -> p s c", c=896),
                    r8_d[h].rearrange("s p c -> p s c"),
                    writes=[f"WS{slot}"])

        def pass3(pairs_a, pairs_b):
            return [(pairs_a[0], pairs_b[0]), (pairs_a[0], pairs_b[1]),
                    (pairs_a[1], pairs_b[0])]

        load_mats(0)
        for h in range(HEADS):
            slot = h % 2
            if h + 1 < HEADS:
                load_mats(h + 1)
            u3h_hi = R2b[:, h * TOK:(h + 1) * TOK].rearrange(
                "p (c j) -> p c j", j=L)
            u3h_lo = R2b[:, 16384 + h * TOK:16384 + (h + 1) * TOK].rearrange(
                "p (c j) -> p c j", j=L)
            y3h = R1[:, h * TOK:(h + 1) * TOK].rearrange(
                "p (c j) -> p c j", j=L)
            mhi = mats_ap(h, "m", 0); mlo = mats_ap(h, "m", 1)
            whi_ = mats_ap(h, "w", 0); wlo_ = mats_ap(h, "w", 1)

            for bank in range(2):
                bres = f"PS_A{bank}"
                mm_i = 0
                total = sum(3 * (i + 1) for i in range(bank * 4, bank * 4 + 4))
                for i in range(bank * 4, bank * 4 + 4):
                    for j in range(i + 1):
                        k = i - j
                        for la, ra in pass3(
                                (mhi[:, k * 128:(k + 1) * 128],
                                 mlo[:, k * 128:(k + 1) * 128]),
                                (u3h_hi[:, :, j], u3h_lo[:, :, j])):
                            bld.emit("tensor", lambda t, _l=la, _r=ra,
                                     _i=i, _s=(mm_i == 0),
                                     _e=(mm_i == total - 1): t.matmul(
                                         PS_A[:, _i * 128:(_i + 1) * 128],
                                         _l, _r, start=_s, stop=_e), sT,
                                     reads=[f"WS{slot}", "R2"],
                                     writes=[bres] if mm_i == 0 else [bres, bres + "a"],
                                     quiet=(mm_i != total - 1))
                            mm_i += 1
                for i in range(bank * 4, bank * 4 + 4):
                    bld.emit("vector", lambda v, _i=i, _y=y3h: v.tensor_copy(
                        _y[:, :, _i], PS_A[:, _i * 128:(_i + 1) * 128]), sV,
                        reads=[bres, bres + "a"], writes=["R1"])

            mm_i = 0
            for j in range(L):
                for la, ra in pass3(
                        (whi_[:, j * 128:(j + 1) * 128],
                         wlo_[:, j * 128:(j + 1) * 128]),
                        (u3h_hi[:, :, j], u3h_lo[:, :, j])):
                    bld.emit("tensor", lambda t, _l=la, _r=ra,
                             _s=(mm_i == 0), _e=(mm_i == 23): t.matmul(
                                 PS_B[:, 0:128], _l, _r,
                                 start=_s, stop=_e), sT,
                             reads=[f"WS{slot}", "R2"],
                             writes=["PS_B"] if mm_i == 0 else ["PS_B", "PS_Ba"],
                             quiet=(mm_i != 23))
                    mm_i += 1
            bld.emit("vector", lambda v: v.tensor_copy(
                s32, PS_B[:, 0:128]), sV, reads=["PS_B", "PS_Ba"],
                writes=["SSMS"])
            bld.emit("vector", lambda v: v.tensor_copy(s_hi, s32), sV,
                     reads=["SSMS"], writes=["SSP"])
            bld.emit("vector", lambda v: v.tensor_tensor(
                s_lo, s32, s_hi, ALU.subtract), sV,
                reads=["SSMS", "SSP"], writes=["SSP"])

            sv3 = SSP[:].rearrange("p (c) -> p c")
            s_hi3 = s_hi.rearrange("p (b m) -> p b m", m=L)
            s_lo3 = s_lo.rearrange("p (b m) -> p b m", m=L)
            r8hi = mats_ap(h, "r8", 0); r8lo = mats_ap(h, "r8", 1)
            mm_i = 0
            total = sum(2 + 3 * j for j in range(L))
            for j in range(L):
                for m in range(j + 1):
                    k = j - m
                    if k == 0:
                        ops = [(IDNB[:], s_hi3[:, :, m]),
                               (IDNB[:], s_lo3[:, :, m])]
                    else:
                        ops = pass3((r8hi[:, (k - 1) * 128:k * 128],
                                     r8lo[:, (k - 1) * 128:k * 128]),
                                    (s_hi3[:, :, m], s_lo3[:, :, m]))
                    for la, ra in ops:
                        bld.emit("tensor", lambda t, _l=la, _r=ra, _j=j,
                                 _s=(mm_i == 0), _e=(mm_i == total - 1):
                                 t.matmul(PS_C[:, _j * 16:(_j + 1) * 16],
                                          _l, _r, start=_s, stop=_e), sT,
                                 reads=[f"WS{slot}", "SSP", "IDNB"],
                                 writes=["PS_C"] if mm_i == 0 else ["PS_C", "PS_Ca"],
                                 quiet=(mm_i != total - 1))
                        mm_i += 1
            hh3 = HH[:, h * 128:(h + 1) * 128].rearrange(
                "p (b j) -> p b j", j=L)
            for j in range(L):
                bld.emit("vector", lambda v, _j=j, _hh=hh3: v.tensor_copy(
                    _hh[:, :, _j], PS_C[:, _j * 16:(_j + 1) * 16]), sV,
                    reads=["PS_C", "PS_Ca"], writes=["NORM"])

        def block_scan(dst, init_hin):
            for b in range(NB):
                mm_i = 0
                n_mm = HEADS * 2 if (b > 0 or init_hin) else HEADS
                for h in range(HEADS):
                    if b > 0 or init_hin:
                        prev = (dst[:, (b - 1) * 16 + h:(b - 1) * 16 + h + 1]
                                if b > 0 else hin[:, h:h + 1])
                        bld.emit("tensor", lambda t, _h=h, _pv=prev,
                                 _s=(mm_i == 0), _e=False: t.matmul(
                                     PS_D[:, _h:_h + 1],
                                     HQ32[:, 2048 + _h * 128:2048 + (_h + 1) * 128],
                                     _pv, start=_s, stop=_e), sT,
                                 reads=["HQ", "SSMS"],
                                 writes=["PS_D"] if mm_i == 0 else ["PS_Da"],
                                 quiet=True)
                        mm_i += 1
                    bld.emit("tensor", lambda t, _h=h, _b=b,
                             _s=(mm_i == 0), _e=(mm_i == n_mm - 1): t.matmul(
                                 PS_D[:, _h:_h + 1], IDN[:],
                                 HH[:, _h * 128 + _b * 8 + 7:
                                    _h * 128 + _b * 8 + 8],
                                 start=_s, stop=_e), sT,
                             reads=["NORM", "IDN", "SSMS"],
                             writes=["PS_D", "PS_Da"] if mm_i == n_mm - 1 else ["PS_Da"],
                             quiet=(mm_i != n_mm - 1))
                    mm_i += 1
                bld.emit("vector", lambda v, _b=b: v.tensor_copy(
                    dst[:, _b * 16:(_b + 1) * 16], PS_D[:, 0:16]), sV,
                    reads=["PS_D", "PS_Da"], writes=["SSMS"])

        block_scan(bstk, init_hin=False)
        import os as _os2
        if _os2.environ.get("KDBG") == "1":
            bld.dma("sync", dS, dbg_d[:, 0:2048], HH[:, :], reads=["NORM"])
            bld.dma("sync", dS, dbg_d[:, 2048:2048 + 384], SSMS[:, 0:384],
                    reads=["SSMS"])

        bld.dma("gpsimd", dCc, cc_in_d.ap()[:, :],
                bstk[:, 15 * 16:16 * 16], reads=["SSMS"], writes=["CCIN"])
        bld.emit("gpsimd", lambda g: g.collective_compute(
            "AllGather", ALU.bypass,
            replica_groups=[list(range(NCORES))],
            ins=[cc_in_d.ap().opt()], outs=[cc_out_d.ap().opt()]), sCC,
            reads=["CCIN"], writes=["CCOUT"])
        bld.dma("gpsimd", dCc,
                hinraw.rearrange("p (r h) -> p r h", h=16),
                cc_out_d.ap().rearrange("r p h -> p r h"),
                reads=["CCOUT"], writes=["SSMS"])
        bld.emit("vector", lambda v: v.tensor_scalar(
            hin, hinraw[:, 0:16], c_mask[:, 0:1], 0.0, ALU.mult, ALU.add),
            sV, reads=["SSMS", "CONST"], writes=["SSMS"])
        for r in range(1, 8):
            bld.emit("vector", lambda v, _r=r: v.tensor_scalar(
                SMALL[:, 0:16], hinraw[:, _r * 16:(_r + 1) * 16],
                c_mask[:, _r:_r + 1], 0.0, ALU.mult, ALU.add), sV,
                reads=["SSMS", "CONST"], writes=["SMALL"])
            bld.emit("vector", lambda v: v.tensor_tensor(
                hin, hin, SMALL[:, 0:16], ALU.add), sV,
                reads=["SSMS", "SMALL"], writes=["SSMS"])

        block_scan(bstk2, init_hin=True)

        load_mats(0, full=False)
        for h in range(HEADS):
            slot = h % 2
            if h + 1 < HEADS:
                load_mats(h + 1, full=False)
            ghi = mats_ap(h, "g", 0); glo = mats_ap(h, "g", 1)
            r8hi = mats_ap(h, "r8", 0); r8lo = mats_ap(h, "r8", 1)
            y3h = R1[:, h * TOK:(h + 1) * TOK].rearrange(
                "p (c j) -> p c j", j=L)
            bld.emit("vector", lambda v, _h=h: v.tensor_copy(
                estk[:, 0:1], hin[:, _h:_h + 1]), sV,
                reads=["SSMS"], writes=["SSMS"])
            bst2s = bstk2.rearrange("p (b h2) -> p b h2", h2=16)
            bld.emit("vector", lambda v, _h=h: v.tensor_copy(
                estk[:, 1:16], bst2s[:, 0:15, _h]), sV,
                reads=["SSMS"], writes=["SSMS"])
            bld.emit("vector", lambda v: v.tensor_copy(e_hi, estk), sV,
                     reads=["SSMS"], writes=["SSP"])
            bld.emit("vector", lambda v: v.tensor_tensor(
                e_lo, estk, e_hi, ALU.subtract), sV,
                reads=["SSMS", "SSP"], writes=["SSP"])
            hh3 = HH[:, h * 128:(h + 1) * 128].rearrange(
                "p (b j) -> p b j", j=L)
            mm_i = 0
            total = 2 + 7 * 4
            for j in range(L):
                _j_last = (j == L - 1)
                if j == 0:
                    ops = [(IDNB[:], e_hi), (IDNB[:], e_lo)]
                else:
                    ops = pass3((r8hi[:, (j - 1) * 128:j * 128],
                                 r8lo[:, (j - 1) * 128:j * 128]),
                                (e_hi, e_lo))
                for la, ra in ops:
                    bld.emit("tensor", lambda t, _l=la, _r=ra, _j=j,
                             _s=(mm_i == 0), _e=False: t.matmul(
                                 PS_E[:, _j * 16:(_j + 1) * 16], _l, _r,
                                 start=_s, stop=_e), sT,
                             reads=[f"WS{slot}", "SSP", "IDNB"],
                             writes=["PS_E"] if mm_i == 0 else ["PS_Ea"],
                             quiet=True)
                    mm_i += 1
                if j >= 1:
                    bld.emit("tensor", lambda t, _j=j, _hh=hh3: t.matmul(
                        PS_E[:, _j * 16:(_j + 1) * 16], IDN[:],
                        _hh[:, :, _j - 1], start=False,
                        stop=(_j == L - 1)), sT,
                        reads=["NORM", "IDN"],
                        writes=["PS_E", "PS_Ea"] if _j_last else ["PS_Ea"],
                        quiet=(not _j_last))
                    mm_i += 1
            h03 = h032.rearrange("p (b j) -> p b j", j=L)
            for j in range(L):
                bld.emit("vector", lambda v, _j=j: v.tensor_copy(
                    h03[:, :, _j], PS_E[:, _j * 16:(_j + 1) * 16]), sV,
                    reads=["PS_E", "PS_Ea"], writes=["SSMS"])
            bld.emit("vector", lambda v: v.tensor_copy(h0_hi, h032), sV,
                     reads=["SSMS"], writes=["SSP"])
            bld.emit("vector", lambda v: v.tensor_tensor(
                h0_lo, h032, h0_hi, ALU.subtract), sV,
                reads=["SSMS", "SSP"], writes=["SSP"])
            for half in range(2):
                mm_i = 0
                for i in range(half * 4, half * 4 + 4):
                    for la, ra in pass3(
                            (ghi[:, i * 128:(i + 1) * 128],
                             glo[:, i * 128:(i + 1) * 128]),
                            (h0_hi, h0_lo)):
                        bld.emit("tensor", lambda t, _l=la, _r=ra, _i=i,
                                 _s=(mm_i == 0), _e=(mm_i == 11): t.matmul(
                                     PS_F[:, (_i % 4) * 128:
                                          (_i % 4 + 1) * 128],
                                     _l, _r, start=_s, stop=_e), sT,
                                 reads=[f"WS{slot}", "SSP"],
                                 writes=["PS_F"] if mm_i == 0 else ["PS_F", "PS_Fa"],
                                 quiet=(mm_i != 11))
                        mm_i += 1
                for i in range(half * 4, half * 4 + 4):
                    bld.emit("vector", lambda v, _i=i, _y=y3h: v.tensor_tensor(
                        _y[:, :, _i], _y[:, :, _i],
                        PS_F[:, (_i % 4) * 128:(_i % 4 + 1) * 128],
                        ALU.add), sV,
                        reads=["PS_F", "PS_Fa", "R1"], writes=["R1"])

        y_hi = R2b[:, 0:16384]
        y_lo = R2b[:, 16384:32768]
        bld.emit("scalar", lambda s: s.copy(y_hi, R1[:, :]), sS,
                 reads=["R1"], writes=["R2"])
        bld.emit("vector", lambda v: v.tensor_tensor(
            y_lo, R1[:, :], y_hi, ALU.subtract), sV,
            reads=["R1", "R2"], writes=["R2"])
        bld.dma("sync", dS, x3[:, :, :],
                spill_xp.rearrange("(tt p) d -> p tt d", p=128),
                reads=["R1"], writes=["R1"],
                extra_deps=[(dSpP, dSpP.n)])
        src_ow = out_w.rearrange("(kc p) m -> p kc m", p=128)
        ow_pairs = {0: stream_w_pair(src_ow[:, :, 0:128], 0)}
        for m in range(FC):
            slot = m % 2
            if m + 1 < FC:
                ow_pairs[m + 1] = stream_w_pair(
                    src_ow[:, :, (m + 1) * 128:(m + 2) * 128], (m + 1) % 2)
            whi, wlo = ow_pairs.pop(m)
            mm_i = 0
            for pa, pb in ((whi, y_hi), (whi, y_lo), (wlo, y_hi)):
                for kc in range(FC):
                    for nt in range(2):
                        ps, pres = ((PS_D, "PS_D"), (PS_E, "PS_E"))[nt]
                        bld.emit("tensor", lambda t, _pa=pa, _pb=pb, _kc=kc,
                                 _p=ps, _s=(mm_i <= 1), _e=(mm_i >= 94),
                                 _nt=nt: t.matmul(
                                     _p[:, :],
                                     _pa[:, _kc * 128:(_kc + 1) * 128],
                                     _pb[:, _kc * TOK + _nt * 512:
                                         _kc * TOK + _nt * 512 + 512],
                                     start=_s, stop=_e), sT,
                                 reads=[f"WS{slot}", "R2"],
                                 writes=[pres] if mm_i <= 1 else [pres, pres + "a"],
                                 quiet=(mm_i < 94))
                        mm_i += 1
            for nt in range(2):
                ps, pres = ((PS_D, "PS_D"), (PS_E, "PS_E"))[nt]
                bld.emit("vector", lambda v, _p=ps, _m=m: v.tensor_scalar(
                    STAGE[:, 0:512], _p[:, :], c_outb[:, _m:_m + 1], 0.0,
                    ALU.add, ALU.add), sV,
                    reads=[pres, pres + "a", "CONST"], writes=["STAGE"])
                for q in range(4):
                    bld.emit("tensor", lambda t, _q=q: t.transpose(
                        PS_F[:, _q * 128:(_q + 1) * 128],
                        STAGE[:, _q * 128:(_q + 1) * 128], IDN[:]), sT,
                        reads=["STAGE", "IDN"], writes=["PS_F"],
                        quiet=(q != 3))
                for q in range(4):
                    tt = nt * 4 + q
                    bld.emit("vector", lambda v, _tt=tt, _m=m, _q=q:
                             v.tensor_tensor(
                                 x3[:, _tt, _m * 128:(_m + 1) * 128],
                                 x3[:, _tt, _m * 128:(_m + 1) * 128],
                                 PS_F[:, _q * 128:(_q + 1) * 128],
                                 ALU.add), sV,
                             reads=["PS_F", "R1"], writes=["R1"])

        for tt in range(NTT):
            ln_normalize(x3[:, tt, :], "R1", par=tt % 2)
            for c4 in range(4):
                for q in range(4):
                    c = c4 * 4 + q
                    bld.emit("tensor", lambda t, _c=c, _q=q: t.transpose(
                        PS_F[:, _q * 128:(_q + 1) * 128],
                        NORM[:, _c * 128:(_c + 1) * 128], IDN[:]), sT,
                        reads=["NORM", "IDN"], writes=["PS_F"],
                        quiet=(q != 3))
                for q in range(4):
                    c = c4 * 4 + q
                    bld.emit("vector", lambda v, _c=c, _q=q: v.tensor_scalar(
                        STAGE[:, 1024 + _q * 128:1024 + (_q + 1) * 128],
                        PS_F[:, _q * 128:(_q + 1) * 128],
                        c_ln2w[:, _c:_c + 1], c_ln2b[:, _c:_c + 1],
                        ALU.mult, ALU.add), sV,
                        reads=["PS_F", "CONST"], writes=["STAGE"])
                mp = c4 % 2
                hi_st = GF[:, 0:512] if mp == 0 else GF[:, 1024:1536]
                lo_st = GF[:, 512:1024] if mp == 0 else JUNKb[:, 0:512]
                hi_res = f"MHH{mp}"
                lo_res = f"MHL{mp}"
                for q in range(4):
                    bld.emit("scalar", lambda s, _q=q, _d=hi_st: s.copy(
                        _d[:, _q * 128:(_q + 1) * 128],
                        STAGE[:, 1024 + _q * 128:1024 + (_q + 1) * 128]), sS,
                        reads=["STAGE"], writes=[hi_res])
                    bld.emit("vector", lambda v, _q=q, _d=lo_st, _s=hi_st:
                             v.tensor_tensor(
                                 _d[:, _q * 128:(_q + 1) * 128],
                                 STAGE[:, 1024 + _q * 128:1024 + (_q + 1) * 128],
                                 _s[:, _q * 128:(_q + 1) * 128], ALU.subtract),
                             sV, reads=["STAGE", hi_res], writes=[lo_res])
                for hl in range(2):
                    bld.dma("gpsimd", (dSpM0, dSpM1, dSpM2, dSpM3)[2 * hl + mp],
                            spill_mh[hl].rearrange(
                                "p (c t) -> p c t", t=TOK)[:, c4 * 4:(c4 + 1) * 4,
                                                           tt * 128:(tt + 1) * 128],
                            (hi_st if hl == 0 else lo_st).rearrange(
                                "p (c t) -> p c t", t=128),
                            reads=[hi_res if hl == 0 else lo_res])
        bld.dma("sync", dSpA,
                spill_xa2.rearrange("(tt p) d -> p tt d", p=128),
                x3[:, :, :], reads=["R1"])

        g_hi = R1b[:, 0:8192]
        g_lo = R1b[:, 8192:16384]
        mh_hi = R1b[:, 16384:24576]
        mh_lo = R1b[:, 24576:32768]
        y3acc = R2
        NSTRIP = 8
        SCH = HC // NSTRIP
        src_w1 = w1.rearrange("(kc p) m -> p kc m", p=128)
        src_w2 = w2.rearrange("(kc p) m -> p kc m", p=128)
        for strip in range(NSTRIP):
            for nt in range(2):
                for hl in range(2):
                    bld.dma("sync", (dR0, dR1)[nt],
                            (mh_hi, mh_lo)[hl].rearrange(
                                "p (c t) -> p c t", t=512),
                            spill_mh[hl].rearrange(
                                "p (c t) -> p c t", t=TOK)[:, :,
                                                           nt * 512:
                                                           nt * 512 + 512],
                            writes=["R1"], extra_deps=[(dSpM0, dSpM0.n), (dSpM1, dSpM1.n), (dSpM2, dSpM2.n), (dSpM3, dSpM3.n)])
                w1_pairs = {0: stream_w_pair(
                    src_w1[:, :, (strip * SCH) * 128:(strip * SCH + 1) * 128], 0)}
                for hcl in range(SCH):
                    hc = strip * SCH + hcl
                    slot = hcl % 2
                    if hcl + 1 < SCH:
                        w1_pairs[hcl + 1] = stream_w_pair(
                            src_w1[:, :, (hc + 1) * 128:(hc + 2) * 128],
                            (hcl + 1) % 2)
                    whi, wlo = w1_pairs.pop(hcl)
                    ps, pres = ((PS_D, "PS_D"), (PS_E, "PS_E"))[hcl % 2]
                    mm_i = 0
                    for pa, pb in ((whi, mh_hi), (whi, mh_lo), (wlo, mh_hi)):
                        for kc in range(FC):
                            bld.emit("tensor", lambda t, _pa=pa, _pb=pb,
                                     _kc=kc, _p=ps, _s=(mm_i == 0),
                                     _e=(mm_i == 47): t.matmul(
                                         _p[:, :],
                                         _pa[:, _kc * 128:(_kc + 1) * 128],
                                         _pb[:, _kc * 512:(_kc + 1) * 512],
                                         start=_s, stop=_e), sT,
                                     reads=[f"WS{slot}", "R1"],
                                     writes=[pres] if mm_i == 0 else [pres, pres + "a"],
                                     quiet=(mm_i != 47))
                            mm_i += 1
                    bld.emit("scalar", lambda s, _p=ps, _hc=hc: s.activation(
                        STAGE[:, 0:512], _p[:, :], AF.Gelu,
                        bias=c_b1[:, _hc:_hc + 1]), sS,
                        reads=[pres, pres + "a", "CONST"], writes=["STAGE"])
                    colg = hcl * TOK + nt * 512
                    bld.emit("scalar", lambda s, _c=colg: s.copy(
                        g_hi[:, _c:_c + 512], STAGE[:, 0:512]), sS,
                        reads=["STAGE"], writes=["R1"])
                    bld.emit("vector", lambda v, _c=colg: v.tensor_tensor(
                        g_lo[:, _c:_c + 512], STAGE[:, 0:512],
                        g_hi[:, _c:_c + 512], ALU.subtract), sV,
                        reads=["STAGE", "R1"], writes=["R1"])
            w2_pairs = {0: stream_w_pair(
                src_w2[:, strip * SCH:(strip + 1) * SCH, 0:128], 0,
                kc_cnt=SCH)}
            for m in range(FC):
                slot = m % 2
                if m + 1 < FC:
                    w2_pairs[m + 1] = stream_w_pair(
                        src_w2[:, strip * SCH:(strip + 1) * SCH,
                               (m + 1) * 128:(m + 2) * 128], (m + 1) % 2,
                        kc_cnt=SCH)
                whi, wlo = w2_pairs.pop(m)
                mm_i = 0
                nmm2 = 6 * SCH
                for pa, pb in ((whi, g_hi), (whi, g_lo), (wlo, g_hi)):
                    for kc in range(SCH):
                        for nt in range(2):
                            ps, pres = ((PS_D, "PS_D"), (PS_E, "PS_E"))[nt]
                            bld.emit("tensor", lambda t, _pa=pa, _pb=pb,
                                     _kc=kc, _p=ps, _s=(mm_i <= 1),
                                     _e=(mm_i >= nmm2 - 2), _nt=nt: t.matmul(
                                         _p[:, :],
                                         _pa[:, _kc * 128:(_kc + 1) * 128],
                                         _pb[:, _kc * TOK + _nt * 512:
                                             _kc * TOK + _nt * 512 + 512],
                                         start=_s, stop=_e), sT,
                                     reads=[f"WS{slot}", "R1"],
                                     writes=[pres] if mm_i <= 1 else [pres, pres + "a"],
                                     quiet=(mm_i < nmm2 - 2))
                            mm_i += 1
                for nt in range(2):
                    ps, pres = ((PS_D, "PS_D"), (PS_E, "PS_E"))[nt]
                    colm = m * TOK + nt * 512
                    if strip == 0:
                        bld.emit("vector", lambda v, _p=ps, _m=m, _c=colm:
                                 v.tensor_scalar(
                                     y3acc[:, _c:_c + 512], _p[:, :],
                                     c_b2[:, _m:_m + 1], 0.0,
                                     ALU.add, ALU.add), sV,
                                 reads=[pres, pres + "a", "CONST"],
                                 writes=["R2"])
                    else:
                        bld.emit("vector", lambda v, _p=ps, _c=colm:
                                 v.tensor_tensor(
                                     y3acc[:, _c:_c + 512],
                                     y3acc[:, _c:_c + 512], _p[:, :],
                                     ALU.add), sV,
                                 reads=[pres, pres + "a", "R2"],
                                 writes=["R2"])

        for tt in range(NTT):
            bld.dma("sync", dR0, NORM[:, :],
                    spill_x.rearrange("(tt p) d -> p tt d", p=128)[:, tt, :],
                    writes=["NORM"], extra_deps=[(dSpX, dSpX.n)])
            for half in range(2):
                bld.dma("sync", dR1, XT2[:, :],
                        spill_xa2.rearrange(
                            "(tt p) d -> p tt d", p=128)[:, tt,
                                                         half * 1024:
                                                         half * 1024 + 1024],
                        writes=["XT2"], extra_deps=[(dSpA, dSpA.n)])
                for c4 in range(half * 2, half * 2 + 2):
                    for q in range(4):
                        c = c4 * 4 + q
                        bld.emit("tensor", lambda t, _c=c, _q=q, _tt=tt:
                                 t.transpose(
                                     PS_F[:, _q * 128:(_q + 1) * 128],
                                     y3acc[:, _c * TOK + _tt * 128:
                                           _c * TOK + _tt * 128 + 128],
                                     IDN[:]), sT,
                                 reads=["R2", "IDN"], writes=["PS_F"],
                                 quiet=(q != 3))
                    for q in range(4):
                        c = c4 * 4 + q
                        bld.emit("vector", lambda v, _c=c, _tt=tt, _h=half:
                                 v.tensor_tensor(
                                     x3[:, _tt, _c * 128:(_c + 1) * 128],
                                     NORM[:, _c * 128:(_c + 1) * 128],
                                     XT2[:, _c * 128 - _h * 1024:
                                         (_c + 1) * 128 - _h * 1024],
                                     ALU.add), sV,
                                 reads=["NORM", "XT2"], writes=["R1"])
                        bld.emit("vector", lambda v, _c=c, _q=q, _tt=tt:
                                 v.tensor_tensor(
                                     x3[:, _tt, _c * 128:(_c + 1) * 128],
                                     x3[:, _tt, _c * 128:(_c + 1) * 128],
                                     PS_F[:, _q * 128:(_q + 1) * 128],
                                     ALU.add), sV,
                                 reads=["PS_F", "R1"], writes=["R1"])


        ffn_sublayer(R1, "R1", R2, "R2")

        vout = bld.dma("sync", dS,
                       out_t.rearrange("(tt p) d -> p tt d", p=128),
                       R1[:].rearrange("p (tt d) -> p tt d", d=D),
                       reads=["R1"])
        bld.barrier_wait("gpsimd", [(dS, vout)])

        with nc.Block() as block:
            @block.sync
            def _(e):
                for fn in bld.q["sync"]:
                    fn(e)

            @block.gpsimd
            def _(e):
                for fn in bld.q["gpsimd"]:
                    fn(e)

            @block.vector
            def _(e):
                for fn in bld.q["vector"]:
                    fn(e)

            @block.scalar
            def _(e):
                for fn in bld.q["scalar"]:
                    fn(e)

            @block.tensor
            def _(e):
                for fn in bld.q["tensor"]:
                    fn(e)

    return nc


def _fm_vec(v):
    return np.ascontiguousarray(v.reshape(-1, 128).T)


def kernel(L0_attn_x, L0_attn_ln1_w, L0_attn_ln1_b, L0_attn_in_w, L0_attn_in_b,
           L0_attn_A, L0_attn_Bm, L0_attn_Cm, L0_attn_Dm, L0_attn_out_w,
           L0_attn_out_b, L0_attn_ln2_w, L0_attn_ln2_b, L0_attn_w1, L0_attn_b1,
           L0_attn_w2, L0_attn_b2, L0_ffn_w_gating, L0_ffn_w_linear,
           _debug_tap=None, _nc=None):
    x = np.ascontiguousarray(np.asarray(L0_attn_x, dtype=np.float32))
    shared = {
        "in_w": np.ascontiguousarray(L0_attn_in_w, dtype=np.float32),
        "out_w": np.ascontiguousarray(L0_attn_out_w, dtype=np.float32),
        "w1": np.ascontiguousarray(L0_attn_w1, dtype=np.float32),
        "w2": np.ascontiguousarray(L0_attn_w2, dtype=np.float32),
        "wg": np.ascontiguousarray(L0_ffn_w_gating, dtype=np.float32),
        "wlin": np.ascontiguousarray(L0_ffn_w_linear, dtype=np.float32),
        "A": np.ascontiguousarray(L0_attn_A, dtype=np.float32),
        "Bm": np.ascontiguousarray(L0_attn_Bm, dtype=np.float32),
        "Cm": np.ascontiguousarray(L0_attn_Cm, dtype=np.float32),
        "Dm": np.ascontiguousarray(L0_attn_Dm, dtype=np.float32),
        "ln1w": _fm_vec(np.asarray(L0_attn_ln1_w, dtype=np.float32)),
        "ln1b": _fm_vec(np.asarray(L0_attn_ln1_b, dtype=np.float32)),
        "ln2w": _fm_vec(np.asarray(L0_attn_ln2_w, dtype=np.float32)),
        "ln2b": _fm_vec(np.asarray(L0_attn_ln2_b, dtype=np.float32)),
        "inb": _fm_vec(np.asarray(L0_attn_in_b, dtype=np.float32)),
        "outb": _fm_vec(np.asarray(L0_attn_out_b, dtype=np.float32)),
        "b1v": _fm_vec(np.asarray(L0_attn_b1, dtype=np.float32)),
        "b2v": _fm_vec(np.asarray(L0_attn_b2, dtype=np.float32)),
    }
    in_maps = []
    for c in range(NCORES):
        b, j = c // 2, c % 2
        m = dict(shared)
        m["x"] = np.ascontiguousarray(x[b, j * TOK:(j + 1) * TOK, :])
        mk = np.zeros((128, 8), dtype=np.float32)
        if j == 1:
            mk[:, c - 1] = 1.0
        m["mask"] = mk
        in_maps.append(m)
    nc = _nc if _nc is not None else build(debug_tap=_debug_tap)
    res = run_bass_kernel_spmd(nc, in_maps, core_ids=list(range(NCORES)))
    out = np.empty((B, T, D), dtype=np.float32)
    for c in range(NCORES):
        b, j = c // 2, c % 2
        out[b, j * TOK:(j + 1) * TOK, :] = res.results[c]["out"]
    if _debug_tap is not None:
        return out, [r["tap"] for r in res.results]
    return out


# version 08 (final)
# speedup vs baseline: 1.0566x; 1.0566x over previous
import numpy as np
from contextlib import ExitStack

import concourse.bass as bass
import concourse.mybir as mybir
from concourse.bass_utils import run_bass_kernel_spmd

F32 = mybir.dt.float32
BF16 = mybir.dt.bfloat16
F8 = mybir.dt.float8e4
AF = mybir.ActivationFunctionType
ALU = mybir.AluOpType

NCORES = 8
B, T, D, HID, HEADS, HD = 4, 2048, 2048, 8192, 16, 128
TOK = 1024
NTT = TOK // 128
FC = D // 128
HC = HID // 128
L = 8
NCH = TOK // L
NB = 16
L2 = 8
EPS = 1e-5

ENGINES = ("sync", "gpsimd", "vector", "scalar", "tensor")
COMPUTE_SEM_ENGINE = {"sT": "tensor", "sV": "vector", "sS": "scalar",
                      "sG": "gpsimd"}


class Sem:
    def __init__(self, ctx, nc, name):
        self.h = ctx.enter_context(nc.semaphore(name))
        self.n = 0
        self.name = name

    def bump(self, k=1):
        self.n += k
        return self.n


class Builder:

    def __init__(self):
        self.q = {e: [] for e in ENGINES}
        self.last_wait = {e: {} for e in ENGINES}
        self.res = {}

    def _deps_for(self, reads, writes):
        deps = {}
        for r in reads:
            e = self.res.get(r)
            if e:
                for s, v in e["w"].items():
                    if deps.get(s, -1) < v:
                        deps[s] = v
        for r in writes:
            e = self.res.get(r)
            if e:
                for part in ("w", "r"):
                    for s, v in e[part].items():
                        if deps.get(s, -1) < v:
                            deps[s] = v
        return deps

    def _record(self, reads, writes, sem, val):
        for r in reads:
            e = self.res.setdefault(r, {"w": {}, "r": {}})
            if e["r"].get(sem, -1) < val:
                e["r"][sem] = val
        for r in writes:
            e = self.res.setdefault(r, {"w": {}, "r": {}})
            e["w"] = {sem: val}
            e["r"] = {}

    def emit(self, eng, make, sem, k=1, reads=(), writes=(), extra_deps=(),
             quiet=False):
        deps = self._deps_for(reads, writes)
        for s, v in extra_deps:
            if deps.get(s, -1) < v:
                deps[s] = v
        eff = []
        lw = self.last_wait[eng]
        for s, v in deps.items():
            if lw.get(id(s), -1) < v:
                eff.append((s.h, v))
                lw[id(s)] = v
        if quiet:
            def fnq(e, _make=make, _eff=tuple(eff)):
                for h, v in _eff:
                    e.wait_ge(h, v)
                _make(e)
            self.q[eng].append(fnq)
            return None
        val = sem.bump(k)

        def fn(e, _make=make, _eff=tuple(eff), _s=sem.h, _k=k):
            for h, v in _eff:
                e.wait_ge(h, v)
            inst = _make(e)
            inst.then_inc(_s, _k)
        self.q[eng].append(fn)
        self._record(reads, writes, sem, val)
        return val

    def dma(self, eng, sem, out_ap, in_ap, reads=(), writes=(), extra_deps=()):
        return self.emit(eng, lambda e, _o=out_ap, _i=in_ap: e.dma_start(
            out=_o, in_=_i), sem, k=16, reads=reads, writes=writes,
            extra_deps=extra_deps)

    def barrier_wait(self, eng, extra_deps):
        lw = self.last_wait[eng]
        eff = []
        for s, v in extra_deps:
            if lw.get(id(s), -1) < v:
                eff.append((s.h, v))
                lw[id(s)] = v
        if eff:
            def fn(e, _eff=tuple(eff)):
                for h, v in _eff:
                    e.wait_ge(h, v)
            self.q[eng].append(fn)


def build(debug_tap=None, upto="all"):
    nc = bass.Bass("TRN2", target_bir_lowering=False)

    x_in = nc.dram_tensor("x", [TOK, D], F32, kind="ExternalInput").ap()
    in_w = nc.dram_tensor("in_w", [D, D], F32, kind="ExternalInput").ap()
    out_w = nc.dram_tensor("out_w", [D, D], F32, kind="ExternalInput").ap()
    w1 = nc.dram_tensor("w1", [D, HID], F32, kind="ExternalInput").ap()
    w2 = nc.dram_tensor("w2", [HID, D], F32, kind="ExternalInput").ap()
    wg = nc.dram_tensor("wg", [2, D, HID], F32, kind="ExternalInput").ap()
    wlin = nc.dram_tensor("wlin", [HID, D], F32, kind="ExternalInput").ap()
    A_in = nc.dram_tensor("A", [HEADS, HD, HD], F32, kind="ExternalInput").ap()
    B_in = nc.dram_tensor("Bm", [HEADS, HD, HD], F32, kind="ExternalInput").ap()
    C_in = nc.dram_tensor("Cm", [HEADS, HD, HD], F32, kind="ExternalInput").ap()
    D_in = nc.dram_tensor("Dm", [HEADS, HD, HD], F32, kind="ExternalInput").ap()
    ln1w = nc.dram_tensor("ln1w", [128, FC], F32, kind="ExternalInput").ap()
    ln1b = nc.dram_tensor("ln1b", [128, FC], F32, kind="ExternalInput").ap()
    ln2w = nc.dram_tensor("ln2w", [128, FC], F32, kind="ExternalInput").ap()
    ln2b = nc.dram_tensor("ln2b", [128, FC], F32, kind="ExternalInput").ap()
    inb = nc.dram_tensor("inb", [128, FC], F32, kind="ExternalInput").ap()
    outb = nc.dram_tensor("outb", [128, FC], F32, kind="ExternalInput").ap()
    b1v = nc.dram_tensor("b1v", [128, HC], F32, kind="ExternalInput").ap()
    b2v = nc.dram_tensor("b2v", [128, FC], F32, kind="ExternalInput").ap()
    mask_in = nc.dram_tensor("mask", [128, 8], F32, kind="ExternalInput").ap()

    out_t = nc.dram_tensor("out", [TOK, D], F32, kind="ExternalOutput").ap()

    spill_x = nc.dram_tensor("spill_x", [TOK, D], F32).ap()
    spill_xp = nc.dram_tensor("spill_xp", [TOK, D], F32).ap()
    spill_xa2 = nc.dram_tensor("spill_xa2", [TOK, D], F32).ap()
    spill_mh = nc.dram_tensor("spill_mh", [2, 128, FC * TOK], BF16).ap()
    mst_d = nc.dram_tensor("mst", [HEADS, 2, 128, L * 128], BF16).ap()
    gst_d = nc.dram_tensor("gst", [HEADS, 2, 128, L * 128], BF16).ap()
    wst_d = nc.dram_tensor("wst", [HEADS, 2, 128, L * 128], BF16).ap()
    r8_d = nc.dram_tensor("r8", [HEADS, 2, 128, 7 * 128], BF16).ap()
    dbg_d = nc.dram_tensor("dbg", [128, 4096], F32).ap()
    cc_in_d = nc.dram_tensor("cc_in", [128, HEADS], F32)
    cc_out_d = nc.dram_tensor("cc_out", [NCORES, 128, HEADS], F32)


    with ExitStack() as ctx:
        def sbt(name, shape, dt):
            return ctx.enter_context(nc.sbuf_tensor(name, shape, dt))

        def psm(name, shape, dt):
            return ctx.enter_context(nc.psum_tensor(name, shape, dt))

        R1 = sbt("R1", [128, 16384], F32)
        R2 = sbt("R2", [128, 16384], F32)
        WARE = sbt("WARE", [128, 8192], F32)
        HQ = sbt("HQ", [128, 16384], F8)
        NORM = sbt("NORM", [128, 2048], F32)
        JUNK = sbt("JUNK", [128, 2048], F8)
        STAT = sbt("STAT", [128, 24], F32)
        IDN = sbt("IDN", [128, 128], F32)
        IDNB = sbt("IDNB", [128, 128], BF16)
        CONST = sbt("CONST", [128, 8 * FC + HC + 8], F32)
        GF = sbt("GF", [128, 1536], BF16)
        STAGE = sbt("STAGE", [128, 1536], F32)
        SMALL = sbt("SMALL", [128, 64], F32)
        SSMS = sbt("SSMS", [128, 960], F32)
        SSP = sbt("SSP", [128, 544], BF16)
        XT2 = sbt("XT2", [128, 1024], F32)

        PS_A = psm("PS_A", [128, 1024], F32)
        PS_B = psm("PS_B", [128, 512], F32)
        PS_C = psm("PS_C", [128, 512], F32)
        PS_D = psm("PS_D", [128, 512], F32)
        PS_E = psm("PS_E", [128, 512], F32)
        PS_F = psm("PS_F", [128, 512], F32)
        PS_G = psm("PS_G", [128, 512], BF16)

        dS = Sem(ctx, nc, "dS")
        dG = Sem(ctx, nc, "dG")
        dX = Sem(ctx, nc, "dX")
        dC = Sem(ctx, nc, "dC")
        dW0 = Sem(ctx, nc, "dW0")
        dW1 = Sem(ctx, nc, "dW1")
        dV0 = Sem(ctx, nc, "dV0")
        dV1 = Sem(ctx, nc, "dV1")
        dM0 = Sem(ctx, nc, "dM0")
        dM1 = Sem(ctx, nc, "dM1")
        dPL = Sem(ctx, nc, "dPL")
        dPS0 = Sem(ctx, nc, "dPS0")
        dPS1 = Sem(ctx, nc, "dPS1")
        dSpX = Sem(ctx, nc, "dSpX")
        dSpP = Sem(ctx, nc, "dSpP")
        dSpA = Sem(ctx, nc, "dSpA")
        dSpM0 = Sem(ctx, nc, "dSpM0")
        dSpM1 = Sem(ctx, nc, "dSpM1")
        dSpM2 = Sem(ctx, nc, "dSpM2")
        dSpM3 = Sem(ctx, nc, "dSpM3")
        dR0 = Sem(ctx, nc, "dR0")
        dR1 = Sem(ctx, nc, "dR1")
        dCc = Sem(ctx, nc, "dCc")
        sCC = Sem(ctx, nc, "sCC")
        sT = Sem(ctx, nc, "sT")
        sV = Sem(ctx, nc, "sV")
        sS = Sem(ctx, nc, "sS")
        sG = Sem(ctx, nc, "sG")
        bld = Builder()

        c_ln1w = CONST[:, 0 * FC:1 * FC]
        c_ln1b = CONST[:, 1 * FC:2 * FC]
        c_ln2w = CONST[:, 2 * FC:3 * FC]
        c_ln2b = CONST[:, 3 * FC:4 * FC]
        c_inb = CONST[:, 4 * FC:5 * FC]
        c_outb = CONST[:, 5 * FC:6 * FC]
        c_b2 = CONST[:, 6 * FC:7 * FC]
        c_b1 = CONST[:, 7 * FC:7 * FC + HC]
        c_mask = CONST[:, 7 * FC + HC:7 * FC + HC + 8]

        bld.dma("sync", dX, R1[:].rearrange("p (tt d) -> p tt d", d=D),
                x_in.rearrange("(tt p) d -> p tt d", p=128), writes=["R1"])
        for dst, srcv in ((c_ln1w, ln1w), (c_ln1b, ln1b), (c_ln2w, ln2w),
                          (c_ln2b, ln2b), (c_inb, inb), (c_outb, outb),
                          (c_b2, b2v), (c_b1, b1v), (c_mask, mask_in)):
            bld.dma("sync", dC, dst, srcv[:, :], writes=["CONST"])
        bld.emit("gpsimd", lambda g: g.iota(
            STAGE[:, 0:128], [[1, 128]], channel_multiplier=0,
            allow_small_or_imprecise_dtypes=True), sG, writes=["STAGE"])
        bld.emit("gpsimd", lambda g: g.iota(
            STAGE[:, 128:256], [[0, 128]], channel_multiplier=1,
            allow_small_or_imprecise_dtypes=True), sG, writes=["STAGE"])
        bld.emit("vector", lambda v: v.tensor_tensor(
            IDN[:], STAGE[:, 0:128], STAGE[:, 128:256], ALU.is_equal), sV,
            reads=["STAGE"], writes=["IDN"])
        bld.emit("vector", lambda v: v.tensor_copy(IDNB[:], IDN[:]), sV,
                 reads=["IDN"], writes=["IDNB"])

        SC16 = 2.0 ** -16
        def ln_normalize(X_ap, xres, out_ap=None, out_res="NORM", par=0):
            if out_ap is None:
                out_ap = NORM[:, :]
            st = STAT[:, par * 12:par * 12 + 12]
            stres = f"STAT{par}"
            jres = f"JUNK{par}" 
            bld.emit("vector", lambda v: v.tensor_reduce(
                st[:, 0:1], X_ap, mybir.AxisListType.X, ALU.add), sV,
                reads=[xres], writes=[stres])
            bld.emit("vector", lambda v: v.tensor_scalar(
                st[:, 2:3], st[:, 0:1], 1.0 / D, 0.0, ALU.mult, ALU.add), sV,
                reads=[stres], writes=[stres])
            bld.emit("vector", lambda v: v.tensor_scalar(
                st[:, 3:4], st[:, 2:3], -SC16, 0.0, ALU.mult, ALU.add), sV,
                reads=[stres], writes=[stres])
            bld.emit("scalar", lambda s: s.activation(
                JUNK[:, :], X_ap, AF.Square, bias=st[:, 3:4], scale=SC16,
                accum_out=st[:, 1:2]), sS,
                reads=[xres, stres], writes=[stres, "JUNK"])
            bld.emit("vector", lambda v: v.tensor_scalar(
                st[:, 4:5], st[:, 1:2], 1.0 / D, EPS * SC16 * SC16,
                ALU.mult, ALU.add), sV,
                reads=[stres], writes=[stres])
            bld.emit("scalar", lambda s: s.activation(
                st[:, 6:7], st[:, 4:5], AF.Sqrt), sS,
                reads=[stres], writes=[stres])
            bld.emit("vector", lambda v: v.reciprocal(
                st[:, 5:6], st[:, 6:7]), sV, reads=[stres], writes=[stres])
            bld.emit("vector", lambda v: v.tensor_scalar(
                st[:, 7:8], st[:, 5:6], SC16, 0.0, ALU.mult, ALU.add), sV,
                reads=[stres], writes=[stres])
            bld.emit("vector", lambda v: v.tensor_tensor(
                st[:, 8:9], st[:, 2:3], st[:, 7:8], ALU.mult), sV,
                reads=[stres], writes=[stres])
            bld.emit("vector", lambda v: v.tensor_scalar(
                st[:, 9:10], st[:, 8:9], -1.0, 0.0, ALU.mult, ALU.add), sV,
                reads=[stres], writes=[stres])
            bld.emit("scalar", lambda s: s.activation(
                out_ap, X_ap, AF.Identity, bias=st[:, 9:10],
                scale=st[:, 7:8]), sS, reads=[xres, stres], writes=[out_res])

        def ffn_sublayer(xbuf, xres, pbuf, prodres):
            x3 = xbuf[:].rearrange("p (tt d) -> p tt d", d=D)
            prod = pbuf[:].bitcast(F8)
            w8 = WARE[:].bitcast(F8)
            for tt in range(NTT):
                ln_normalize(x3[:, tt, :], xres, par=tt % 2)
                for c4 in range(4):
                    for q in range(4):
                        c = c4 * 4 + q
                        bld.emit("tensor", lambda t, _c=c, _q=q: t.transpose(
                            PS_F[:, _q * 128:(_q + 1) * 128],
                            NORM[:, _c * 128:(_c + 1) * 128], IDN[:]), sT,
                            reads=["NORM", "IDN"], writes=["PS_F"],
                            quiet=(q != 3))
                    for q in range(4):
                        c = c4 * 4 + q
                        col = c * TOK + tt * 128
                        bld.emit("vector", lambda v, _col=col, _q=q: v.tensor_copy(
                            HQ[:, _col:_col + 128],
                            PS_F[:, _q * 128:(_q + 1) * 128]), sV,
                            reads=["PS_F"], writes=["HQ"])

            slot_cols = FC * 512
            src_g = wg.rearrange("g (kc p) m -> g p kc m", p=128)

            def load_gate_slab(mg, slot):
                for g in range(2):
                    bld.dma("gpsimd", (dW0, dW1)[slot],
                            w8[:, (2 * slot + g) * slot_cols:
                               (2 * slot + g + 1) * slot_cols]
                            .rearrange("p (kc m) -> p kc m", m=512),
                            src_g[g, :, :, mg * 512:(mg + 1) * 512],
                            writes=[f"WS{slot}"])

            load_gate_slab(0, 0)
            for mg in range(HID // 512):
                slot = mg % 2
                if mg + 1 < HID // 512:
                    load_gate_slab(mg + 1, (mg + 1) % 2)
                for mc4 in range(4):
                    hc_idx = mg * 4 + mc4
                    for nt in range(2):
                        psg, gres = (PS_A[:, 0:512], "PS_A0") if nt == 0 else (PS_B[:, :], "PS_B")
                        psf, fres = (PS_A[:, 512:1024], "PS_A1") if nt == 0 else (PS_C[:, :], "PS_C")
                        for g, ps, qres in ((0, psg, gres), (1, psf, fres)):
                            for kc in range(FC):
                                lhs = w8[:, (2 * slot + g) * slot_cols + kc * 512
                                         + mc4 * 128:(2 * slot + g) * slot_cols
                                         + kc * 512 + mc4 * 128 + 128]
                                rhs = HQ[:, kc * TOK + nt * 512:
                                         kc * TOK + nt * 512 + 512]
                                bld.emit("tensor", lambda t, _l=lhs, _r=rhs,
                                         _p=ps, _k=kc: t.matmul(
                                             _p, _l, _r, start=(_k == 0),
                                             stop=(_k == FC - 1)), sT,
                                         reads=[f"WS{slot}", "HQ"],
                                         writes=[qres] if kc == 0 else [qres, qres + "acc"],
                                         quiet=(kc != FC - 1))
                        bld.emit("scalar", lambda s, _p=psg: s.activation(
                            GF[:, 0:512], _p, AF.Gelu_apprx_tanh), sS,
                            reads=[gres, gres + "acc"], writes=["GF0"])
                        bld.emit("vector", lambda v, _p=psf: v.tensor_copy(
                            GF[:, 512:1024], _p), sV,
                            reads=[fres, fres + "acc"], writes=["GF1"])
                        bld.emit("vector", lambda v: v.tensor_tensor(
                            GF[:, 1024:1536], GF[:, 0:512], GF[:, 512:1024],
                            ALU.mult), sV, reads=["GF0", "GF1"], writes=["GF2"])
                        col = hc_idx * TOK + nt * 512
                        bld.emit("vector", lambda v, _col=col: v.tensor_copy(
                            prod[:, _col:_col + 512], GF[:, 1024:1536]), sV,
                            reads=["GF2"], writes=[prodres])

            lin_cols = HC * 256
            src_l = wlin.rearrange("(kc p) m -> p kc m", p=128)

            def load_lin_slab(mg2, slot):
                bld.dma("gpsimd", (dW0, dW1)[slot],
                        w8[:, slot * lin_cols:(slot + 1) * lin_cols]
                        .rearrange("p (kc m) -> p kc m", m=256),
                        src_l[:, :, mg2 * 256:(mg2 + 1) * 256],
                        writes=[f"WS{slot}"])

            load_lin_slab(0, 0)
            for mg2 in range(D // 256):
                slot = mg2 % 2
                if mg2 + 1 < D // 256:
                    load_lin_slab(mg2 + 1, (mg2 + 1) % 2)
                for mc2 in range(2):
                    fc_idx = mg2 * 2 + mc2
                    for nt in range(2):
                        ps, bres = ((PS_D, "PS_D"), (PS_E, "PS_E"))[nt]
                        for kc in range(HC):
                            lhs = w8[:, slot * lin_cols + kc * 256 + mc2 * 128:
                                     slot * lin_cols + kc * 256 + mc2 * 128 + 128]
                            rhs = prod[:, kc * TOK + nt * 512:
                                       kc * TOK + nt * 512 + 512]
                            bld.emit("tensor", lambda t, _l=lhs, _r=rhs,
                                     _p=ps, _k=kc: t.matmul(
                                         _p[:, :], _l, _r, start=(_k == 0),
                                         stop=(_k == HC - 1)), sT,
                                     reads=[f"WS{slot}", prodres],
                                     writes=[bres] if kc == 0 else [bres, bres + "acc"],
                                     quiet=(kc != HC - 1))
                        bld.emit("vector", lambda v, _p=ps: v.tensor_copy(
                            GF[:, 0:512], _p[:, :]), sV,
                            reads=[bres, bres + "acc"], writes=["GF0"])
                        for q in range(4):
                            bld.emit("tensor", lambda t, _q=q: t.transpose(
                                PS_G[:, _q * 128:(_q + 1) * 128],
                                GF[:, _q * 128:(_q + 1) * 128], IDNB[:]), sT,
                                reads=["GF0", "IDNB"], writes=["PS_G"],
                                quiet=(q != 3))
                        for q in range(4):
                            bld.emit("scalar", lambda s, _q=q: s.activation(
                                STAGE[:, _q * 128:(_q + 1) * 128],
                                PS_G[:, _q * 128:(_q + 1) * 128],
                                AF.Identity, scale=0.5), sS,
                                reads=["PS_G"], writes=["STAGE"])
                        for q in range(4):
                            tt = nt * 4 + q
                            bld.emit("vector", lambda v, _tt=tt, _fc=fc_idx,
                                     _q=q: v.tensor_tensor(
                                         x3[:, _tt, _fc * 128:(_fc + 1) * 128],
                                         x3[:, _tt, _fc * 128:(_fc + 1) * 128],
                                         STAGE[:, _q * 128:(_q + 1) * 128],
                                         ALU.add), sV,
                                     reads=["STAGE", xres], writes=[xres])

        ffn_sublayer(R1, "R1", R2, "R2")

        R1b = R1[:].bitcast(BF16)
        R2b = R2[:].bitcast(BF16)
        HQ32 = HQ[:].bitcast(F32)
        HQb = HQ[:]
        HQbf = HQ[:].bitcast(BF16)
        WAREb = WARE[:].bitcast(BF16)
        JUNKb = JUNK[:].bitcast(BF16)

        def prologue():
            a_t = STAGE[:, 0:128]; b_t = STAGE[:, 128:256]
            c_t = STAGE[:, 256:384]; d_t = STAGE[:, 384:512]
            CTB = STAGE[:, 512:640]; N8 = STAGE[:, 768:896]
            RSTK = HQ32[:, 0:1024]; GSTK = HQ32[:, 1024:2048]
            SC8 = HQ32[:, 2048:4096]
            WSTKF = NORM[:, 0:1024]; MSTKF = NORM[:, 1024:2048]

            def chain_mm(lhs_ap, rhs_ap, out_sb_ap, pidx, extra=(),
                         group=None):
                ps, pres = ((PS_D, "PS_D"), (PS_E, "PS_E"))[pidx % 2]
                if group is None:
                    bld.emit("tensor", lambda t, _l=lhs_ap, _r=rhs_ap,
                             _p=ps: t.matmul(_p[:, 0:128], _l, _r,
                                             start=True, stop=True), sT,
                             reads=list(extra) + ["STAGE", "HQ", "NORM"],
                             writes=[pres])
                else:
                    for gi, (la, ra) in enumerate(group):
                        bld.emit("tensor", lambda t, _l=la, _r=ra, _p=ps,
                                 _s=(gi == 0), _e=(gi == len(group) - 1):
                                 t.matmul(_p[:, 0:128], _l, _r,
                                          start=_s, stop=_e), sT,
                                 reads=list(extra) + ["STAGE", "HQ", "NORM"],
                                 writes=[pres])
                bld.emit("scalar", lambda s, _o=out_sb_ap, _p=ps: s.copy(
                    _o, _p[:, 0:128]), sS, reads=[pres],
                    writes=[out_sb_ap_res])

            src_abcd = [A_in, B_in, C_in, D_in]
            for h in range(HEADS):
                for i4, srcm in enumerate(src_abcd):
                    bld.dma("sync", dPL, STAGE[:, i4 * 128:(i4 + 1) * 128],
                            srcm[h, :, :], writes=["STAGE"])
                pidx = 0
                global out_sb_ap_res
                out_sb_ap_res = "STAGE"
                chain_mm(c_t, IDN[:], CTB, pidx, extra=["IDN"]); pidx += 1
                out_sb_ap_res = "HQ"
                bld.emit("vector", lambda v: v.tensor_copy(
                    RSTK[:, 0:128], IDN[:]), sV, reads=["IDN"], writes=["HQ"])
                chain_mm(a_t, CTB, GSTK[:, 0:128], pidx); pidx += 1
                for m in range(1, 8):
                    chain_mm(a_t, RSTK[:, (m - 1) * 128:m * 128],
                             RSTK[:, m * 128:(m + 1) * 128], pidx); pidx += 1
                    chain_mm(a_t, GSTK[:, (m - 1) * 128:m * 128],
                             GSTK[:, m * 128:(m + 1) * 128], pidx); pidx += 1
                out_sb_ap_res = "NORM"
                chain_mm(None, None, MSTKF[:, 0:128], pidx,
                         group=[(b_t, CTB), (d_t, IDN[:])],
                         extra=["IDN"]); pidx += 1
                for j in range(8):
                    chain_mm(b_t, RSTK[:, (7 - j) * 128:(8 - j) * 128],
                             WSTKF[:, j * 128:(j + 1) * 128], pidx); pidx += 1
                    if j >= 1:
                        chain_mm(b_t, GSTK[:, (j - 1) * 128:j * 128],
                                 MSTKF[:, j * 128:(j + 1) * 128], pidx); pidx += 1
                def split_store(stk_ap, stk_res, dst, n):
                    bld.emit("vector", lambda v, _s=stk_ap, _n=n: v.tensor_copy(
                        GF[:, 0:_n], _s), sV, reads=[stk_res],
                        writes=["GF0"])
                    bld.dma("sync", dPS0, dst[0, :, :], GF[:, 0:n],
                            reads=["GF0"])
                    bld.emit("vector", lambda v, _s=stk_ap, _n=n: v.tensor_tensor(
                        JUNKb[:, 0:_n], _s, GF[:, 0:_n], ALU.subtract), sV,
                        reads=[stk_res, "GF0"], writes=["JUNK"])
                    bld.dma("sync", dPS1, dst[1, :, :], JUNKb[:, 0:n],
                            reads=["JUNK"])
                split_store(WSTKF[:, 0:1024], "NORM", wst_d[h], 1024)
                split_store(MSTKF[:, 0:1024], "NORM", mst_d[h], 1024)
                split_store(GSTK[:, 0:1024], "HQ", gst_d[h], 1024)
                out_sb_ap_res = "STAGE"
                chain_mm(RSTK[:, 7 * 128:8 * 128], a_t, N8, pidx); pidx += 1
                out_sb_ap_res = "NORM"
                chain_mm(a_t, RSTK[:, 7 * 128:8 * 128],
                         NORM[:, 0:128], pidx); pidx += 1
                for m2 in range(2, 8):
                    chain_mm(N8, NORM[:, (m2 - 2) * 128:(m2 - 1) * 128],
                             NORM[:, (m2 - 1) * 128:m2 * 128], pidx); pidx += 1
                out_sb_ap_res = "HQ"
                chain_mm(N8, NORM[:, 6 * 128:7 * 128],
                         SC8[:, h * 128:(h + 1) * 128], pidx); pidx += 1
                split_store(NORM[:, 0:896], "NORM", r8_d[h], 896)

        prologue()

        x3 = R1[:].rearrange("p (tt d) -> p tt d", d=D)
        xp3 = R2[:].rearrange("p (tt d) -> p tt d", d=D)
        for tt in range(NTT):
            ln_normalize(x3[:, tt, :], "R1", out_ap=xp3[:, tt, :],
                         out_res="R2", par=tt % 2)
        bld.dma("sync", dSpX, spill_x.rearrange("(tt p) d -> p tt d", p=128),
                x3[:, :, :], reads=["R1"])

        h1_hi = R1b[:, 0:16384]
        h1_lo = R1b[:, 16384:32768]
        for tt in range(NTT):
            ln_normalize(xp3[:, tt, :], "R2", par=tt % 2)
            for c4 in range(4):
                for q in range(4):
                    c = c4 * 4 + q
                    bld.emit("tensor", lambda t, _c=c, _q=q: t.transpose(
                        PS_F[:, _q * 128:(_q + 1) * 128],
                        NORM[:, _c * 128:(_c + 1) * 128], IDN[:]), sT,
                        reads=["NORM", "IDN"], writes=["PS_F"],
                        quiet=(q != 3))
                for q in range(4):
                    c = c4 * 4 + q
                    col = c * TOK + tt * 128
                    bld.emit("vector", lambda v, _c=c, _q=q: v.tensor_scalar(
                        STAGE[:, 1024 + _q * 128:1024 + (_q + 1) * 128],
                        PS_F[:, _q * 128:(_q + 1) * 128],
                        c_ln1w[:, _c:_c + 1], c_ln1b[:, _c:_c + 1],
                        ALU.mult, ALU.add), sV,
                        reads=["PS_F", "CONST"], writes=["STAGE"])
                for q in range(4):
                    c = c4 * 4 + q
                    col = c * TOK + tt * 128
                    bld.emit("vector", lambda v, _col=col, _q=q: v.tensor_copy(
                        h1_hi[:, _col:_col + 128],
                        STAGE[:, 1024 + _q * 128:1024 + (_q + 1) * 128]), sV,
                        reads=["STAGE"], writes=["R1"])
                    bld.emit("vector", lambda v, _col=col, _q=q: v.tensor_tensor(
                        h1_lo[:, _col:_col + 128],
                        STAGE[:, 1024 + _q * 128:1024 + (_q + 1) * 128],
                        h1_hi[:, _col:_col + 128], ALU.subtract), sV,
                        reads=["STAGE", "R1"], writes=["R1"])
        bld.dma("sync", dSpP, spill_xp.rearrange("(tt p) d -> p tt d", p=128),
                xp3[:, :, :], reads=["R2"])

        def stream_w_pair(src3_ap, slot, kc_cnt=FC):
            n = kc_cnt * 128
            WSTG = WARE[:, slot * 4096:slot * 4096 + n]
            whi = WAREb[:, slot * 8192 + 4096:slot * 8192 + 4096 + n]
            wlo = WAREb[:, slot * 8192 + 6144:slot * 8192 + 6144 + n]
            bld.dma("sync", (dV0, dV1)[slot],
                    WSTG.rearrange("p (kc m) -> p kc m", m=128), src3_ap,
                    writes=[f"WS{slot}"])
            bld.emit("vector", lambda v: v.tensor_copy(whi, WSTG), sV,
                     reads=[f"WS{slot}"], writes=[f"WS{slot}"])
            bld.emit("vector", lambda v: v.tensor_tensor(
                wlo, WSTG, whi, ALU.subtract), sV,
                reads=[f"WS{slot}"], writes=[f"WS{slot}"])
            return whi, wlo

        u_hi = R2b[:, 0:16384]
        u_lo = R2b[:, 16384:32768]
        src_iw = in_w.rearrange("(kc p) m -> p kc m", p=128)
        iw_pairs = {0: stream_w_pair(src_iw[:, :, 0:128], 0)}
        for h in range(HEADS):
            slot = h % 2
            if h + 1 < HEADS:
                iw_pairs[h + 1] = stream_w_pair(
                    src_iw[:, :, (h + 1) * 128:(h + 2) * 128], (h + 1) % 2)
            whi, wlo = iw_pairs.pop(h)
            for nt in range(2):
                ps, pres = ((PS_D, "PS_D"), (PS_E, "PS_E"))[nt]
                mm_i = 0
                for pa, pb in ((whi, h1_hi), (whi, h1_lo), (wlo, h1_hi)):
                    for kc in range(FC):
                        bld.emit("tensor", lambda t, _l=pa[:, kc * 128:(kc + 1) * 128]
                                 if False else None, _pa=pa, _pb=pb, _kc=kc,
                                 _p=ps, _s=(mm_i == 0), _e=(mm_i == 47),
                                 _nt=nt: t.matmul(
                                     _p[:, :],
                                     _pa[:, _kc * 128:(_kc + 1) * 128],
                                     _pb[:, _kc * TOK + _nt * 512:
                                         _kc * TOK + _nt * 512 + 512],
                                     start=_s, stop=_e), sT,
                                 reads=[f"WS{slot}", "R1"],
                                 writes=[pres] if mm_i == 0 else [pres, pres + "a"],
                                 quiet=(mm_i != 47))
                        mm_i += 1
                bld.emit("vector", lambda v, _p=ps, _h=h: v.tensor_scalar(
                    STAGE[:, 0:512], _p[:, :], c_inb[:, _h:_h + 1], 0.0,
                    ALU.add, ALU.add), sV,
                    reads=[pres, pres + "a", "CONST"], writes=["STAGE"])
                colu = h * TOK + nt * 512
                bld.emit("vector", lambda v, _c=colu: v.tensor_copy(
                    u_hi[:, _c:_c + 512], STAGE[:, 0:512]), sV,
                    reads=["STAGE"], writes=["R2"])
                bld.emit("vector", lambda v, _c=colu: v.tensor_tensor(
                    u_lo[:, _c:_c + 512], STAGE[:, 0:512],
                    u_hi[:, _c:_c + 512], ALU.subtract), sV,
                    reads=["STAGE", "R2"], writes=["R2"])

        bld.barrier_wait("sync", [(dPS0, dPS0.n), (dPS1, dPS1.n)])
        y32 = R1
        HH = NORM
        s32 = SSMS[:, 0:128]
        bstk = SSMS[:, 128:384]
        bstk2 = SSMS[:, 384:640]
        hin = SSMS[:, 640:656]
        hinraw = SSMS[:, 656:784]
        estk = SSMS[:, 784:800]
        h032 = SSMS[:, 800:928]
        s_hi = SSP[:, 0:128]; s_lo = SSP[:, 128:256]
        e_hi = SSP[:, 256:272]; e_lo = SSP[:, 272:288]
        h0_hi = SSP[:, 288:416]; h0_lo = SSP[:, 416:544]

        def mats_ap(h, which, hl):
            base = (h % 2) * 8192
            if which == "r8":
                off = 6144 + hl * 896
                return WAREb[:, base + off:base + off + 896]
            off = {"m": 0, "g": 2048, "w": 4096}[which] + hl * 1024
            return WAREb[:, base + off:base + off + 1024]

        def load_mats(h, full=True):
            slot = h % 2
            sem = (dM0, dM1)[slot]
            pairs = [("m", mst_d), ("g", gst_d), ("w", wst_d)] if full else [("g", gst_d)]
            for which, dsrc in pairs:
                base = (h % 2) * 8192 + {"m": 0, "g": 2048, "w": 4096}[which]
                bld.dma("sync", sem,
                        WAREb[:, base:base + 2048].rearrange(
                            "p (s c) -> p s c", c=1024),
                        dsrc[h].rearrange("s p c -> p s c"),
                        writes=[f"WS{slot}"])
            base = (h % 2) * 8192 + 6144
            bld.dma("sync", sem,
                    WAREb[:, base:base + 1792].rearrange(
                        "p (s c) -> p s c", c=896),
                    r8_d[h].rearrange("s p c -> p s c"),
                    writes=[f"WS{slot}"])

        def pass3(pairs_a, pairs_b):
            return [(pairs_a[0], pairs_b[0]), (pairs_a[0], pairs_b[1]),
                    (pairs_a[1], pairs_b[0])]

        load_mats(0)
        for h in range(HEADS):
            slot = h % 2
            if h + 1 < HEADS:
                load_mats(h + 1)
            u3h_hi = R2b[:, h * TOK:(h + 1) * TOK].rearrange(
                "p (c j) -> p c j", j=L)
            u3h_lo = R2b[:, 16384 + h * TOK:16384 + (h + 1) * TOK].rearrange(
                "p (c j) -> p c j", j=L)
            y3h = R1[:, h * TOK:(h + 1) * TOK].rearrange(
                "p (c j) -> p c j", j=L)
            mhi = mats_ap(h, "m", 0); mlo = mats_ap(h, "m", 1)
            whi_ = mats_ap(h, "w", 0); wlo_ = mats_ap(h, "w", 1)

            for bank in range(2):
                bres = f"PS_A{bank}"
                mm_i = 0
                total = sum(3 * (i + 1) for i in range(bank * 4, bank * 4 + 4))
                for i in range(bank * 4, bank * 4 + 4):
                    for j in range(i + 1):
                        k = i - j
                        for la, ra in pass3(
                                (mhi[:, k * 128:(k + 1) * 128],
                                 mlo[:, k * 128:(k + 1) * 128]),
                                (u3h_hi[:, :, j], u3h_lo[:, :, j])):
                            bld.emit("tensor", lambda t, _l=la, _r=ra,
                                     _i=i, _s=(mm_i == 0),
                                     _e=(mm_i == total - 1): t.matmul(
                                         PS_A[:, _i * 128:(_i + 1) * 128],
                                         _l, _r, start=_s, stop=_e), sT,
                                     reads=[f"WS{slot}", "R2"],
                                     writes=[bres] if mm_i == 0 else [bres, bres + "a"],
                                     quiet=(mm_i != total - 1))
                            mm_i += 1
                for i in range(bank * 4, bank * 4 + 4):
                    bld.emit("vector", lambda v, _i=i, _y=y3h: v.tensor_copy(
                        _y[:, :, _i], PS_A[:, _i * 128:(_i + 1) * 128]), sV,
                        reads=[bres, bres + "a"], writes=["R1"])

            mm_i = 0
            for j in range(L):
                for la, ra in pass3(
                        (whi_[:, j * 128:(j + 1) * 128],
                         wlo_[:, j * 128:(j + 1) * 128]),
                        (u3h_hi[:, :, j], u3h_lo[:, :, j])):
                    bld.emit("tensor", lambda t, _l=la, _r=ra,
                             _s=(mm_i == 0), _e=(mm_i == 23): t.matmul(
                                 PS_B[:, 0:128], _l, _r,
                                 start=_s, stop=_e), sT,
                             reads=[f"WS{slot}", "R2"],
                             writes=["PS_B"] if mm_i == 0 else ["PS_B", "PS_Ba"],
                             quiet=(mm_i != 23))
                    mm_i += 1
            bld.emit("vector", lambda v: v.tensor_copy(
                s32, PS_B[:, 0:128]), sV, reads=["PS_B", "PS_Ba"],
                writes=["SSMS"])
            bld.emit("vector", lambda v: v.tensor_copy(s_hi, s32), sV,
                     reads=["SSMS"], writes=["SSP"])
            bld.emit("vector", lambda v: v.tensor_tensor(
                s_lo, s32, s_hi, ALU.subtract), sV,
                reads=["SSMS", "SSP"], writes=["SSP"])

            sv3 = SSP[:].rearrange("p (c) -> p c")
            s_hi3 = s_hi.rearrange("p (b m) -> p b m", m=L)
            s_lo3 = s_lo.rearrange("p (b m) -> p b m", m=L)
            r8hi = mats_ap(h, "r8", 0); r8lo = mats_ap(h, "r8", 1)
            mm_i = 0
            total = sum(2 + 3 * j for j in range(L))
            for j in range(L):
                for m in range(j + 1):
                    k = j - m
                    if k == 0:
                        ops = [(IDNB[:], s_hi3[:, :, m]),
                               (IDNB[:], s_lo3[:, :, m])]
                    else:
                        ops = pass3((r8hi[:, (k - 1) * 128:k * 128],
                                     r8lo[:, (k - 1) * 128:k * 128]),
                                    (s_hi3[:, :, m], s_lo3[:, :, m]))
                    for la, ra in ops:
                        bld.emit("tensor", lambda t, _l=la, _r=ra, _j=j,
                                 _s=(mm_i == 0), _e=(mm_i == total - 1):
                                 t.matmul(PS_C[:, _j * 16:(_j + 1) * 16],
                                          _l, _r, start=_s, stop=_e), sT,
                                 reads=[f"WS{slot}", "SSP", "IDNB"],
                                 writes=["PS_C"] if mm_i == 0 else ["PS_C", "PS_Ca"],
                                 quiet=(mm_i != total - 1))
                        mm_i += 1
            hh3 = HH[:, h * 128:(h + 1) * 128].rearrange(
                "p (b j) -> p b j", j=L)
            for j in range(L):
                bld.emit("vector", lambda v, _j=j, _hh=hh3: v.tensor_copy(
                    _hh[:, :, _j], PS_C[:, _j * 16:(_j + 1) * 16]), sV,
                    reads=["PS_C", "PS_Ca"], writes=["NORM"])

        def block_scan(dst, init_hin):
            for b in range(NB):
                mm_i = 0
                n_mm = HEADS * 2 if (b > 0 or init_hin) else HEADS
                for h in range(HEADS):
                    if b > 0 or init_hin:
                        prev = (dst[:, (b - 1) * 16 + h:(b - 1) * 16 + h + 1]
                                if b > 0 else hin[:, h:h + 1])
                        bld.emit("tensor", lambda t, _h=h, _pv=prev,
                                 _s=(mm_i == 0), _e=False: t.matmul(
                                     PS_D[:, _h:_h + 1],
                                     HQ32[:, 2048 + _h * 128:2048 + (_h + 1) * 128],
                                     _pv, start=_s, stop=_e), sT,
                                 reads=["HQ", "SSMS"],
                                 writes=["PS_D"] if mm_i == 0 else ["PS_Da"],
                                 quiet=True)
                        mm_i += 1
                    bld.emit("tensor", lambda t, _h=h, _b=b,
                             _s=(mm_i == 0), _e=(mm_i == n_mm - 1): t.matmul(
                                 PS_D[:, _h:_h + 1], IDN[:],
                                 HH[:, _h * 128 + _b * 8 + 7:
                                    _h * 128 + _b * 8 + 8],
                                 start=_s, stop=_e), sT,
                             reads=["NORM", "IDN", "SSMS"],
                             writes=["PS_D", "PS_Da"] if mm_i == n_mm - 1 else ["PS_Da"],
                             quiet=(mm_i != n_mm - 1))
                    mm_i += 1
                bld.emit("vector", lambda v, _b=b: v.tensor_copy(
                    dst[:, _b * 16:(_b + 1) * 16], PS_D[:, 0:16]), sV,
                    reads=["PS_D", "PS_Da"], writes=["SSMS"])

        block_scan(bstk, init_hin=False)
        import os as _os2
        if _os2.environ.get("KDBG") == "1":
            bld.dma("sync", dS, dbg_d[:, 0:2048], HH[:, :], reads=["NORM"])
            bld.dma("sync", dS, dbg_d[:, 2048:2048 + 384], SSMS[:, 0:384],
                    reads=["SSMS"])

        bld.dma("gpsimd", dCc, cc_in_d.ap()[:, :],
                bstk[:, 15 * 16:16 * 16], reads=["SSMS"], writes=["CCIN"])
        bld.emit("gpsimd", lambda g: g.collective_compute(
            "AllGather", ALU.bypass,
            replica_groups=[list(range(NCORES))],
            ins=[cc_in_d.ap().opt()], outs=[cc_out_d.ap().opt()]), sCC,
            reads=["CCIN"], writes=["CCOUT"])
        bld.dma("gpsimd", dCc,
                hinraw.rearrange("p (r h) -> p r h", h=16),
                cc_out_d.ap().rearrange("r p h -> p r h"),
                reads=["CCOUT"], writes=["SSMS"])
        bld.emit("vector", lambda v: v.tensor_scalar(
            hin, hinraw[:, 0:16], c_mask[:, 0:1], 0.0, ALU.mult, ALU.add),
            sV, reads=["SSMS", "CONST"], writes=["SSMS"])
        for r in range(1, 8):
            bld.emit("vector", lambda v, _r=r: v.tensor_scalar(
                SMALL[:, 0:16], hinraw[:, _r * 16:(_r + 1) * 16],
                c_mask[:, _r:_r + 1], 0.0, ALU.mult, ALU.add), sV,
                reads=["SSMS", "CONST"], writes=["SMALL"])
            bld.emit("vector", lambda v: v.tensor_tensor(
                hin, hin, SMALL[:, 0:16], ALU.add), sV,
                reads=["SSMS", "SMALL"], writes=["SSMS"])

        block_scan(bstk2, init_hin=True)

        load_mats(0, full=False)
        for h in range(HEADS):
            slot = h % 2
            if h + 1 < HEADS:
                load_mats(h + 1, full=False)
            ghi = mats_ap(h, "g", 0); glo = mats_ap(h, "g", 1)
            r8hi = mats_ap(h, "r8", 0); r8lo = mats_ap(h, "r8", 1)
            y3h = R1[:, h * TOK:(h + 1) * TOK].rearrange(
                "p (c j) -> p c j", j=L)
            bld.emit("vector", lambda v, _h=h: v.tensor_copy(
                estk[:, 0:1], hin[:, _h:_h + 1]), sV,
                reads=["SSMS"], writes=["SSMS"])
            bst2s = bstk2.rearrange("p (b h2) -> p b h2", h2=16)
            bld.emit("vector", lambda v, _h=h: v.tensor_copy(
                estk[:, 1:16], bst2s[:, 0:15, _h]), sV,
                reads=["SSMS"], writes=["SSMS"])
            bld.emit("vector", lambda v: v.tensor_copy(e_hi, estk), sV,
                     reads=["SSMS"], writes=["SSP"])
            bld.emit("vector", lambda v: v.tensor_tensor(
                e_lo, estk, e_hi, ALU.subtract), sV,
                reads=["SSMS", "SSP"], writes=["SSP"])
            hh3 = HH[:, h * 128:(h + 1) * 128].rearrange(
                "p (b j) -> p b j", j=L)
            mm_i = 0
            total = 2 + 7 * 4
            for j in range(L):
                _j_last = (j == L - 1)
                if j == 0:
                    ops = [(IDNB[:], e_hi), (IDNB[:], e_lo)]
                else:
                    ops = pass3((r8hi[:, (j - 1) * 128:j * 128],
                                 r8lo[:, (j - 1) * 128:j * 128]),
                                (e_hi, e_lo))
                for la, ra in ops:
                    bld.emit("tensor", lambda t, _l=la, _r=ra, _j=j,
                             _s=(mm_i == 0), _e=False: t.matmul(
                                 PS_E[:, _j * 16:(_j + 1) * 16], _l, _r,
                                 start=_s, stop=_e), sT,
                             reads=[f"WS{slot}", "SSP", "IDNB"],
                             writes=["PS_E"] if mm_i == 0 else ["PS_Ea"],
                             quiet=True)
                    mm_i += 1
                if j >= 1:
                    bld.emit("tensor", lambda t, _j=j, _hh=hh3: t.matmul(
                        PS_E[:, _j * 16:(_j + 1) * 16], IDN[:],
                        _hh[:, :, _j - 1], start=False,
                        stop=(_j == L - 1)), sT,
                        reads=["NORM", "IDN"],
                        writes=["PS_E", "PS_Ea"] if _j_last else ["PS_Ea"],
                        quiet=(not _j_last))
                    mm_i += 1
            h03 = h032.rearrange("p (b j) -> p b j", j=L)
            for j in range(L):
                bld.emit("vector", lambda v, _j=j: v.tensor_copy(
                    h03[:, :, _j], PS_E[:, _j * 16:(_j + 1) * 16]), sV,
                    reads=["PS_E", "PS_Ea"], writes=["SSMS"])
            bld.emit("vector", lambda v: v.tensor_copy(h0_hi, h032), sV,
                     reads=["SSMS"], writes=["SSP"])
            bld.emit("vector", lambda v: v.tensor_tensor(
                h0_lo, h032, h0_hi, ALU.subtract), sV,
                reads=["SSMS", "SSP"], writes=["SSP"])
            for half in range(2):
                mm_i = 0
                for i in range(half * 4, half * 4 + 4):
                    for la, ra in pass3(
                            (ghi[:, i * 128:(i + 1) * 128],
                             glo[:, i * 128:(i + 1) * 128]),
                            (h0_hi, h0_lo)):
                        bld.emit("tensor", lambda t, _l=la, _r=ra, _i=i,
                                 _s=(mm_i == 0), _e=(mm_i == 11): t.matmul(
                                     PS_F[:, (_i % 4) * 128:
                                          (_i % 4 + 1) * 128],
                                     _l, _r, start=_s, stop=_e), sT,
                                 reads=[f"WS{slot}", "SSP"],
                                 writes=["PS_F"] if mm_i == 0 else ["PS_F", "PS_Fa"],
                                 quiet=(mm_i != 11))
                        mm_i += 1
                for i in range(half * 4, half * 4 + 4):
                    bld.emit("vector", lambda v, _i=i, _y=y3h: v.tensor_tensor(
                        _y[:, :, _i], _y[:, :, _i],
                        PS_F[:, (_i % 4) * 128:(_i % 4 + 1) * 128],
                        ALU.add), sV,
                        reads=["PS_F", "PS_Fa", "R1"], writes=["R1"])

        y_hi = R2b[:, 0:16384]
        y_lo = R2b[:, 16384:32768]
        bld.emit("vector", lambda v: v.tensor_copy(y_hi, R1[:, :]), sV,
                 reads=["R1"], writes=["R2"])
        bld.emit("vector", lambda v: v.tensor_tensor(
            y_lo, R1[:, :], y_hi, ALU.subtract), sV,
            reads=["R1", "R2"], writes=["R2"])
        bld.dma("sync", dS, x3[:, :, :],
                spill_xp.rearrange("(tt p) d -> p tt d", p=128),
                reads=["R1"], writes=["R1"],
                extra_deps=[(dSpP, dSpP.n)])
        src_ow = out_w.rearrange("(kc p) m -> p kc m", p=128)
        ow_pairs = {0: stream_w_pair(src_ow[:, :, 0:128], 0)}
        for m in range(FC):
            slot = m % 2
            if m + 1 < FC:
                ow_pairs[m + 1] = stream_w_pair(
                    src_ow[:, :, (m + 1) * 128:(m + 2) * 128], (m + 1) % 2)
            whi, wlo = ow_pairs.pop(m)
            for nt in range(2):
                ps, pres = ((PS_D, "PS_D"), (PS_E, "PS_E"))[nt]
                mm_i = 0
                for pa, pb in ((whi, y_hi), (whi, y_lo), (wlo, y_hi)):
                    for kc in range(FC):
                        bld.emit("tensor", lambda t, _pa=pa, _pb=pb, _kc=kc,
                                 _p=ps, _s=(mm_i == 0), _e=(mm_i == 47),
                                 _nt=nt: t.matmul(
                                     _p[:, :],
                                     _pa[:, _kc * 128:(_kc + 1) * 128],
                                     _pb[:, _kc * TOK + _nt * 512:
                                         _kc * TOK + _nt * 512 + 512],
                                     start=_s, stop=_e), sT,
                                 reads=[f"WS{slot}", "R2"],
                                 writes=[pres] if mm_i == 0 else [pres, pres + "a"],
                                 quiet=(mm_i != 47))
                        mm_i += 1
                bld.emit("vector", lambda v, _p=ps, _m=m: v.tensor_scalar(
                    STAGE[:, 0:512], _p[:, :], c_outb[:, _m:_m + 1], 0.0,
                    ALU.add, ALU.add), sV,
                    reads=[pres, pres + "a", "CONST"], writes=["STAGE"])
                for q in range(4):
                    bld.emit("tensor", lambda t, _q=q: t.transpose(
                        PS_F[:, _q * 128:(_q + 1) * 128],
                        STAGE[:, _q * 128:(_q + 1) * 128], IDN[:]), sT,
                        reads=["STAGE", "IDN"], writes=["PS_F"],
                        quiet=(q != 3))
                for q in range(4):
                    tt = nt * 4 + q
                    bld.emit("vector", lambda v, _tt=tt, _m=m, _q=q:
                             v.tensor_tensor(
                                 x3[:, _tt, _m * 128:(_m + 1) * 128],
                                 x3[:, _tt, _m * 128:(_m + 1) * 128],
                                 PS_F[:, _q * 128:(_q + 1) * 128],
                                 ALU.add), sV,
                             reads=["PS_F", "R1"], writes=["R1"])

        for tt in range(NTT):
            ln_normalize(x3[:, tt, :], "R1", par=tt % 2)
            for c4 in range(4):
                for q in range(4):
                    c = c4 * 4 + q
                    bld.emit("tensor", lambda t, _c=c, _q=q: t.transpose(
                        PS_F[:, _q * 128:(_q + 1) * 128],
                        NORM[:, _c * 128:(_c + 1) * 128], IDN[:]), sT,
                        reads=["NORM", "IDN"], writes=["PS_F"],
                        quiet=(q != 3))
                for q in range(4):
                    c = c4 * 4 + q
                    bld.emit("vector", lambda v, _c=c, _q=q: v.tensor_scalar(
                        STAGE[:, 1024 + _q * 128:1024 + (_q + 1) * 128],
                        PS_F[:, _q * 128:(_q + 1) * 128],
                        c_ln2w[:, _c:_c + 1], c_ln2b[:, _c:_c + 1],
                        ALU.mult, ALU.add), sV,
                        reads=["PS_F", "CONST"], writes=["STAGE"])
                mp = c4 % 2
                hi_st = GF[:, 0:512] if mp == 0 else GF[:, 1024:1536]
                lo_st = GF[:, 512:1024] if mp == 0 else JUNKb[:, 0:512]
                hi_res = "GF0" if mp == 0 else "GF2"
                lo_res = "GF1" if mp == 0 else "JUNK"
                for q in range(4):
                    bld.emit("vector", lambda v, _q=q, _d=hi_st: v.tensor_copy(
                        _d[:, _q * 128:(_q + 1) * 128],
                        STAGE[:, 1024 + _q * 128:1024 + (_q + 1) * 128]), sV,
                        reads=["STAGE"], writes=[hi_res])
                    bld.emit("vector", lambda v, _q=q, _d=lo_st, _s=hi_st:
                             v.tensor_tensor(
                                 _d[:, _q * 128:(_q + 1) * 128],
                                 STAGE[:, 1024 + _q * 128:1024 + (_q + 1) * 128],
                                 _s[:, _q * 128:(_q + 1) * 128], ALU.subtract),
                             sV, reads=["STAGE", hi_res], writes=[lo_res])
                for hl in range(2):
                    bld.dma("gpsimd", (dSpM0, dSpM1, dSpM2, dSpM3)[2 * hl + mp],
                            spill_mh[hl].rearrange(
                                "p (c t) -> p c t", t=TOK)[:, c4 * 4:(c4 + 1) * 4,
                                                           tt * 128:(tt + 1) * 128],
                            (hi_st if hl == 0 else lo_st).rearrange(
                                "p (c t) -> p c t", t=128),
                            reads=[hi_res if hl == 0 else lo_res])
        bld.dma("sync", dSpA,
                spill_xa2.rearrange("(tt p) d -> p tt d", p=128),
                x3[:, :, :], reads=["R1"])

        g_hi = R1b[:, 0:8192]
        g_lo = R1b[:, 8192:16384]
        mh_hi = R1b[:, 16384:24576]
        mh_lo = R1b[:, 24576:32768]
        y3acc = R2
        NSTRIP = 8
        SCH = HC // NSTRIP
        src_w1 = w1.rearrange("(kc p) m -> p kc m", p=128)
        src_w2 = w2.rearrange("(kc p) m -> p kc m", p=128)
        for strip in range(NSTRIP):
            for nt in range(2):
                for hl in range(2):
                    bld.dma("sync", (dR0, dR1)[nt],
                            (mh_hi, mh_lo)[hl].rearrange(
                                "p (c t) -> p c t", t=512),
                            spill_mh[hl].rearrange(
                                "p (c t) -> p c t", t=TOK)[:, :,
                                                           nt * 512:
                                                           nt * 512 + 512],
                            writes=["R1"], extra_deps=[(dSpM0, dSpM0.n), (dSpM1, dSpM1.n), (dSpM2, dSpM2.n), (dSpM3, dSpM3.n)])
                w1_pairs = {0: stream_w_pair(
                    src_w1[:, :, (strip * SCH) * 128:(strip * SCH + 1) * 128], 0)}
                for hcl in range(SCH):
                    hc = strip * SCH + hcl
                    slot = hcl % 2
                    if hcl + 1 < SCH:
                        w1_pairs[hcl + 1] = stream_w_pair(
                            src_w1[:, :, (hc + 1) * 128:(hc + 2) * 128],
                            (hcl + 1) % 2)
                    whi, wlo = w1_pairs.pop(hcl)
                    ps, pres = ((PS_D, "PS_D"), (PS_E, "PS_E"))[hcl % 2]
                    mm_i = 0
                    for pa, pb in ((whi, mh_hi), (whi, mh_lo), (wlo, mh_hi)):
                        for kc in range(FC):
                            bld.emit("tensor", lambda t, _pa=pa, _pb=pb,
                                     _kc=kc, _p=ps, _s=(mm_i == 0),
                                     _e=(mm_i == 47): t.matmul(
                                         _p[:, :],
                                         _pa[:, _kc * 128:(_kc + 1) * 128],
                                         _pb[:, _kc * 512:(_kc + 1) * 512],
                                         start=_s, stop=_e), sT,
                                     reads=[f"WS{slot}", "R1"],
                                     writes=[pres] if mm_i == 0 else [pres, pres + "a"],
                                     quiet=(mm_i != 47))
                            mm_i += 1
                    bld.emit("scalar", lambda s, _p=ps, _hc=hc: s.activation(
                        STAGE[:, 0:512], _p[:, :], AF.Gelu,
                        bias=c_b1[:, _hc:_hc + 1]), sS,
                        reads=[pres, pres + "a", "CONST"], writes=["STAGE"])
                    colg = hcl * TOK + nt * 512
                    bld.emit("vector", lambda v, _c=colg: v.tensor_copy(
                        g_hi[:, _c:_c + 512], STAGE[:, 0:512]), sV,
                        reads=["STAGE"], writes=["R1"])
                    bld.emit("vector", lambda v, _c=colg: v.tensor_tensor(
                        g_lo[:, _c:_c + 512], STAGE[:, 0:512],
                        g_hi[:, _c:_c + 512], ALU.subtract), sV,
                        reads=["STAGE", "R1"], writes=["R1"])
            w2_pairs = {0: stream_w_pair(
                src_w2[:, strip * SCH:(strip + 1) * SCH, 0:128], 0,
                kc_cnt=SCH)}
            for m in range(FC):
                slot = m % 2
                if m + 1 < FC:
                    w2_pairs[m + 1] = stream_w_pair(
                        src_w2[:, strip * SCH:(strip + 1) * SCH,
                               (m + 1) * 128:(m + 2) * 128], (m + 1) % 2,
                        kc_cnt=SCH)
                whi, wlo = w2_pairs.pop(m)
                for nt in range(2):
                    ps, pres = ((PS_D, "PS_D"), (PS_E, "PS_E"))[nt]
                    mm_i = 0
                    nmm = 3 * SCH
                    for pa, pb in ((whi, g_hi), (whi, g_lo), (wlo, g_hi)):
                        for kc in range(SCH):
                            bld.emit("tensor", lambda t, _pa=pa, _pb=pb,
                                     _kc=kc, _p=ps, _s=(mm_i == 0),
                                     _e=(mm_i == nmm - 1), _nt=nt: t.matmul(
                                         _p[:, :],
                                         _pa[:, _kc * 128:(_kc + 1) * 128],
                                         _pb[:, _kc * TOK + _nt * 512:
                                             _kc * TOK + _nt * 512 + 512],
                                         start=_s, stop=_e), sT,
                                     reads=[f"WS{slot}", "R1"],
                                     writes=[pres] if mm_i == 0 else [pres, pres + "a"],
                                     quiet=(mm_i != nmm - 1))
                            mm_i += 1
                    colm = m * TOK + nt * 512
                    if strip == 0:
                        bld.emit("vector", lambda v, _p=ps, _m=m, _c=colm:
                                 v.tensor_scalar(
                                     y3acc[:, _c:_c + 512], _p[:, :],
                                     c_b2[:, _m:_m + 1], 0.0,
                                     ALU.add, ALU.add), sV,
                                 reads=[pres, pres + "a", "CONST"],
                                 writes=["R2"])
                    else:
                        bld.emit("vector", lambda v, _p=ps, _c=colm:
                                 v.tensor_tensor(
                                     y3acc[:, _c:_c + 512],
                                     y3acc[:, _c:_c + 512], _p[:, :],
                                     ALU.add), sV,
                                 reads=[pres, pres + "a", "R2"],
                                 writes=["R2"])

        for tt in range(NTT):
            bld.dma("sync", dR0, NORM[:, :],
                    spill_x.rearrange("(tt p) d -> p tt d", p=128)[:, tt, :],
                    writes=["NORM"], extra_deps=[(dSpX, dSpX.n)])
            for half in range(2):
                bld.dma("sync", dR1, XT2[:, :],
                        spill_xa2.rearrange(
                            "(tt p) d -> p tt d", p=128)[:, tt,
                                                         half * 1024:
                                                         half * 1024 + 1024],
                        writes=["XT2"], extra_deps=[(dSpA, dSpA.n)])
                for c4 in range(half * 2, half * 2 + 2):
                    for q in range(4):
                        c = c4 * 4 + q
                        bld.emit("tensor", lambda t, _c=c, _q=q, _tt=tt:
                                 t.transpose(
                                     PS_F[:, _q * 128:(_q + 1) * 128],
                                     y3acc[:, _c * TOK + _tt * 128:
                                           _c * TOK + _tt * 128 + 128],
                                     IDN[:]), sT,
                                 reads=["R2", "IDN"], writes=["PS_F"],
                                 quiet=(q != 3))
                    for q in range(4):
                        c = c4 * 4 + q
                        bld.emit("vector", lambda v, _c=c, _tt=tt, _h=half:
                                 v.tensor_tensor(
                                     x3[:, _tt, _c * 128:(_c + 1) * 128],
                                     NORM[:, _c * 128:(_c + 1) * 128],
                                     XT2[:, _c * 128 - _h * 1024:
                                         (_c + 1) * 128 - _h * 1024],
                                     ALU.add), sV,
                                 reads=["NORM", "XT2"], writes=["R1"])
                        bld.emit("vector", lambda v, _c=c, _q=q, _tt=tt:
                                 v.tensor_tensor(
                                     x3[:, _tt, _c * 128:(_c + 1) * 128],
                                     x3[:, _tt, _c * 128:(_c + 1) * 128],
                                     PS_F[:, _q * 128:(_q + 1) * 128],
                                     ALU.add), sV,
                                 reads=["PS_F", "R1"], writes=["R1"])


        ffn_sublayer(R1, "R1", R2, "R2")

        vout = bld.dma("sync", dS,
                       out_t.rearrange("(tt p) d -> p tt d", p=128),
                       R1[:].rearrange("p (tt d) -> p tt d", d=D),
                       reads=["R1"])
        bld.barrier_wait("gpsimd", [(dS, vout)])

        with nc.Block() as block:
            @block.sync
            def _(e):
                for fn in bld.q["sync"]:
                    fn(e)

            @block.gpsimd
            def _(e):
                for fn in bld.q["gpsimd"]:
                    fn(e)

            @block.vector
            def _(e):
                for fn in bld.q["vector"]:
                    fn(e)

            @block.scalar
            def _(e):
                for fn in bld.q["scalar"]:
                    fn(e)

            @block.tensor
            def _(e):
                for fn in bld.q["tensor"]:
                    fn(e)

    return nc


def _fm_vec(v):
    return np.ascontiguousarray(v.reshape(-1, 128).T)


def kernel(L0_attn_x, L0_attn_ln1_w, L0_attn_ln1_b, L0_attn_in_w, L0_attn_in_b,
           L0_attn_A, L0_attn_Bm, L0_attn_Cm, L0_attn_Dm, L0_attn_out_w,
           L0_attn_out_b, L0_attn_ln2_w, L0_attn_ln2_b, L0_attn_w1, L0_attn_b1,
           L0_attn_w2, L0_attn_b2, L0_ffn_w_gating, L0_ffn_w_linear,
           _debug_tap=None, _nc=None):
    x = np.ascontiguousarray(np.asarray(L0_attn_x, dtype=np.float32))
    shared = {
        "in_w": np.ascontiguousarray(L0_attn_in_w, dtype=np.float32),
        "out_w": np.ascontiguousarray(L0_attn_out_w, dtype=np.float32),
        "w1": np.ascontiguousarray(L0_attn_w1, dtype=np.float32),
        "w2": np.ascontiguousarray(L0_attn_w2, dtype=np.float32),
        "wg": np.ascontiguousarray(L0_ffn_w_gating, dtype=np.float32),
        "wlin": np.ascontiguousarray(L0_ffn_w_linear, dtype=np.float32),
        "A": np.ascontiguousarray(L0_attn_A, dtype=np.float32),
        "Bm": np.ascontiguousarray(L0_attn_Bm, dtype=np.float32),
        "Cm": np.ascontiguousarray(L0_attn_Cm, dtype=np.float32),
        "Dm": np.ascontiguousarray(L0_attn_Dm, dtype=np.float32),
        "ln1w": _fm_vec(np.asarray(L0_attn_ln1_w, dtype=np.float32)),
        "ln1b": _fm_vec(np.asarray(L0_attn_ln1_b, dtype=np.float32)),
        "ln2w": _fm_vec(np.asarray(L0_attn_ln2_w, dtype=np.float32)),
        "ln2b": _fm_vec(np.asarray(L0_attn_ln2_b, dtype=np.float32)),
        "inb": _fm_vec(np.asarray(L0_attn_in_b, dtype=np.float32)),
        "outb": _fm_vec(np.asarray(L0_attn_out_b, dtype=np.float32)),
        "b1v": _fm_vec(np.asarray(L0_attn_b1, dtype=np.float32)),
        "b2v": _fm_vec(np.asarray(L0_attn_b2, dtype=np.float32)),
    }
    in_maps = []
    for c in range(NCORES):
        b, j = c // 2, c % 2
        m = dict(shared)
        m["x"] = np.ascontiguousarray(x[b, j * TOK:(j + 1) * TOK, :])
        mk = np.zeros((128, 8), dtype=np.float32)
        if j == 1:
            mk[:, c - 1] = 1.0
        m["mask"] = mk
        in_maps.append(m)
    nc = _nc if _nc is not None else build(debug_tap=_debug_tap)
    res = run_bass_kernel_spmd(nc, in_maps, core_ids=list(range(NCORES)))
    out = np.empty((B, T, D), dtype=np.float32)
    for c in range(NCORES):
        b, j = c // 2, c % 2
        out[b, j * TOK:(j + 1) * TOK, :] = res.results[c]["out"]
    if _debug_tap is not None:
        return out, [r["tap"] for r in res.results]
    return out
